# version 03 (final)
# baseline (speedup 1.0000x reference)
import contextlib

import numpy as np
import ml_dtypes
import concourse.bass as bass
import concourse.mybir as mybir
from concourse.bass_utils import run_bass_kernel_spmd

F32 = mybir.dt.float32
BF16 = mybir.dt.bfloat16
I32 = mybir.dt.int32
AF = mybir.ActivationFunctionType
ALU = mybir.AluOpType
AX = mybir.AxisListType.X

P = 128
T = 2048
D = 1024
NCH = D // P
NTT = T // P
TS = 512
NTS = T // TS

MAGIC = float(np.float32(1.5 * 2 ** 23))
N_CORES = 8

ENGINES = ("tensor", "vector", "scalar", "gpsimd", "sync")


class Step:
    __slots__ = ("sem", "val", "eng", "kind")

    def __init__(self, sem, val, eng, kind):
        self.sem = sem
        self.val = val
        self.eng = eng
        self.kind = kind


class Buf:
    def __init__(self, h=None):
        self.h = h
        self.w = None
        self.r = []


class Prog:
    def __init__(self, nc, es):
        self.nc = nc
        self.es = es
        self.steps = {e: [] for e in ENGINES}
        self.engsem = {e: es.enter_context(nc.semaphore("se_" + e)) for e in ENGINES}
        self.dmasem = {e: es.enter_context(nc.semaphore("sd_" + e)) for e in ENGINES}
        self.engcnt = {e: 0 for e in ENGINES}
        self.dmacnt = {e: 0 for e in ENGINES}
        self.last = {}
        self.fence_deps = []
        self.collcnt = 0

    def _mkstep(self, eng, reads, writes, extra, fn, dma, coll=False):
        deps = []
        for b in reads:
            if b.w is not None:
                deps.append(b.w)
        for b in writes:
            if b.w is not None:
                deps.append(b.w)
            deps.extend(b.r)
        deps.extend(extra)
        deps.extend(self.fence_deps)
        if coll:
            sem = self.es.enter_context(self.nc.semaphore(f"scoll{self.collcnt}"))
            self.collcnt += 1
            st = Step(sem, 1, eng, "coll")
        elif dma:
            self.dmacnt[eng] += 1
            st = Step(self.dmasem[eng], 16 * self.dmacnt[eng], eng, "dma")
        else:
            self.engcnt[eng] += 1
            st = Step(self.engsem[eng], self.engcnt[eng], eng, "eng")
        eff = []
        for d in deps:
            if d is None:
                continue
            if d.kind == "dma":
                cum = self.dmacnt[d.eng] - (1 if (dma and d.eng == eng) else 0)
                eff.append((d.sem, 16 * cum))
            else:
                eff.append((d.sem, d.val))
        self.steps[eng].append((eff, fn, dma, st))
        for b in reads:
            b.r.append(st)
        for b in writes:
            b.w = st
            b.r = []
        self.last[(eng, st.kind)] = st
        return st

    def fence(self):
        self.fence_deps = list(self.last.values())

    def step(self, eng, reads=(), writes=(), after=(), dma=False, coll=False):
        def deco(fn):
            return self._mkstep(eng, list(reads), list(writes), list(after),
                                fn, dma, coll)
        return deco

    def emit(self, block):
        for eng in ENGINES:
            steps = self.steps[eng]
            if not steps:
                continue

            def body(e, steps=steps, eng=eng):
                waited = {}
                for eff, fn, dma, st in steps:
                    for sem, val in eff:
                        if waited.get(id(sem), -1) >= val:
                            continue
                        e.wait_ge(sem, val)
                        waited[id(sem)] = val
                    last = fn(e)
                    assert last is not None, f"step on {eng} returned None"
                    if isinstance(last, (list, tuple)):
                        last = last[-1]
                    if st.kind == "coll":
                        last.then_inc(st.sem)
                    else:
                        last.then_inc(st.sem, 16 if dma else 1)
            getattr(block, eng)(body)


class Build:
    def __init__(self, nc, es, prog):
        self.nc = nc
        self.es = es
        self.prog = prog

    def vgap(self, e):
        e.tensor_scalar(self.gap.h[:, :], self.gap.h[:, :], 1.0, None, ALU.mult)

    def tap(self, buf, n=1):
        pass

    def sb(self, name, shape, dtype, stack=None):
        return Buf((stack or self.es).enter_context(
            self.nc.sbuf_tensor(name, shape, dtype)))

    def ps(self, name, shape, dtype, stack=None):
        return Buf((stack or self.es).enter_context(
            self.nc.psum_tensor(name, shape, dtype)))



def emit_entry(b, x_hbm, c_hbm):
    prog = b.prog
    b.X = [b.sb(f"X{ic}", [P, T], F32) for ic in range(NCH)]
    b.ident = b.sb("sb_ident", [P, P], F32)
    b.identb = b.sb("sb_identb", [P, P], BF16)
    b.ones_col = b.sb("ones_col", [P, 1], F32)
    b.ones_row = b.sb("ones_row", [1, P], F32)
    b.gap = b.sb("gap11", [1, 1], F32)
    prog.step("gpsimd", writes=[b.gap])(lambda e: e.memset(b.gap.h[:, :], 1.0))

    prog.step("sync", writes=[b.ident], dma=True)(
        lambda e: e.dma_start(out=b.ident.h[:, :], in_=c_hbm["ident_f32"][:, :]))
    prog.step("sync", writes=[b.identb], dma=True)(
        lambda e: e.dma_start(out=b.identb.h[:, :], in_=c_hbm["ident_bf16"][:, :]))
    prog.step("gpsimd", writes=[b.ones_col])(
        lambda e: e.memset(b.ones_col.h[:, :], 1.0))
    prog.step("gpsimd", writes=[b.ones_row])(
        lambda e: e.memset(b.ones_row.h[:, :], 1.0))

    with contextlib.ExitStack() as ph:
        xtok = [b.sb(f"xtok{i}", [P, D], F32, ph) for i in range(NTT)]
        pst = [b.ps(f"e_pst{k}", [P, TS], F32, ph) for k in range(2)]
        for i in range(NTT):
            prog.step("sync", writes=[xtok[i]], dma=True)(
                lambda e, i=i: e.dma_start(out=xtok[i].h[:, :],
                                           in_=x_hbm[i * P:(i + 1) * P, :]))
        k = 0
        for ic in range(NCH):
            for grp in range(NTT // 4):
                pp = pst[k % 2]
                k += 1
                tcs = [grp * 4 + j for j in range(4)]

                @prog.step("tensor", reads=[b.ident] + [xtok[tc] for tc in tcs],
                           writes=[pp])
                def _(e, ic=ic, tcs=tcs, pp=pp):
                    for j, tc in enumerate(tcs):
                        ins = e.transpose(pp.h[:, j * P:(j + 1) * P],
                                          xtok[tc].h[:, ic * P:(ic + 1) * P],
                                          b.ident.h[:, :])
                    return ins

                @prog.step("vector", reads=[pp], writes=[b.X[ic]])
                def _(e, ic=ic, grp=grp, pp=pp):
                    return e.tensor_copy(b.X[ic].h[:, grp * TS:(grp + 1) * TS],
                                         pp.h[:, :])


def emit_exit(b, y_hbm):
    prog = b.prog
    with contextlib.ExitStack() as ph:
        ytok = [b.sb(f"ytok{i}", [P, D], F32, ph) for i in range(2)]
        pst = [b.ps(f"x_pst{k}", [P, TS], F32, ph) for k in range(2)]
        last = None
        for tt in range(NTT):
            yt = ytok[tt % 2]
            for grp in range(2):
                pp = pst[grp]
                ics = [grp * 4 + j for j in range(4)]

                @prog.step("tensor", reads=[b.ident] + [b.X[ic] for ic in ics],
                           writes=[pp])
                def _(e, tt=tt, ics=ics, pp=pp):
                    for j, ic in enumerate(ics):
                        ins = e.transpose(pp.h[:, j * P:(j + 1) * P],
                                          b.X[ic].h[:, tt * P:(tt + 1) * P],
                                          b.ident.h[:, :])
                    return ins

                @prog.step("vector", reads=[pp], writes=[yt])
                def _(e, yt=yt, grp=grp, pp=pp):
                    return e.tensor_copy(yt.h[:, grp * TS:(grp + 1) * TS], pp.h[:, :])

            last = prog.step("sync", reads=[yt], dma=True)(
                lambda e, yt=yt, tt=tt: e.dma_start(
                    out=y_hbm[tt * P:(tt + 1) * P, :], in_=yt.h[:, :]))

        @prog.step("sync", after=[last])
        def _(e):
            return e.nop()


def emit_ln_stats(b, ph, tag, negmean, rstd, row_a, row_b):
    nc, prog = b.nc, b.prog
    X = b.X
    with contextlib.ExitStack() as st:
        sq = [b.sb(tag + f"_sq{k}", [P, T], F32, st) for k in range(2)]
        ps_s = b.ps(tag + "_pss", [1, T], F32, st)
        ps_q = b.ps(tag + "_psq", [1, T], F32, st)

        for ic in range(NCH):
            sqb = sq[ic % 2]
            prog.step("scalar", reads=[X[ic]], writes=[sqb])(
                lambda e, ic=ic, sqb=sqb: e.activation(sqb.h[:, :], X[ic].h[:, :],
                                                       AF.Square))

            @prog.step("tensor", reads=[b.ones_col, X[ic], sqb],
                       writes=[ps_s, ps_q])
            def _(e, ic=ic, sqb=sqb):
                for sl in range(NTS):
                    e.matmul(ps_s.h[:, sl * TS:(sl + 1) * TS], b.ones_col.h[:, :],
                             X[ic].h[:, sl * TS:(sl + 1) * TS],
                             start=(ic == 0), stop=(ic == NCH - 1))
                    ins = e.matmul(ps_q.h[:, sl * TS:(sl + 1) * TS],
                                   b.ones_col.h[:, :],
                                   sqb.h[:, sl * TS:(sl + 1) * TS],
                                   start=(ic == 0), stop=(ic == NCH - 1))
                return ins

        @prog.step("vector", reads=[ps_s, ps_q], writes=[negmean, row_a, row_b])
        def _(e):
            e.tensor_scalar(negmean.h[0:1, 0:T], ps_s.h[:, :], -1.0 / D, None,
                            ALU.mult)
            e.tensor_scalar(row_a.h[0:1, 0:T], ps_q.h[:, :], 1.0 / D, None, ALU.mult)
            e.tensor_tensor(row_b.h[0:1, 0:T], negmean.h[0:1, 0:T],
                            negmean.h[0:1, 0:T], ALU.mult)
            e.tensor_tensor(row_a.h[0:1, 0:T], row_a.h[0:1, 0:T],
                            row_b.h[0:1, 0:T], ALU.subtract)
            e.tensor_scalar(row_a.h[0:1, 0:T], row_a.h[0:1, 0:T], 1e-5, None,
                            ALU.add)
            b.vgap(e)
            return e.reciprocal(row_a.h[0:1, 0:T], row_a.h[0:1, 0:T])

        @prog.step("scalar", reads=[row_a], writes=[rstd])
        def _(e):
            return e.activation(rstd.h[0:1, 0:T], row_a.h[0:1, 0:T], AF.Sqrt)


def emit_bcast(b, psb, row, out_sbuf, name):
    prog = b.prog
    for sl in range(NTS):
        @prog.step("tensor", reads=[b.ones_row, row], writes=[psb])
        def _(e, sl=sl):
            return e.matmul(psb.h[:, :], b.ones_row.h[:, :],
                            row.h[0:1, sl * TS:(sl + 1) * TS],
                            start=True, stop=True)

        @prog.step("vector", reads=[psb], writes=[out_sbuf])
        def _(e, sl=sl):
            return e.tensor_copy(out_sbuf.h[:, sl * TS:(sl + 1) * TS], psb.h[:, :])


def emit_col_to_row(b, ph, colbuf, ncols, row, name):
    nc, prog = b.nc, b.prog
    pst = b.ps(name + "_pst", [P, P], F32, ph)
    tb = b.sb(name + "_tb", [ncols, P], F32, ph)

    @prog.step("tensor", reads=[colbuf, b.ident], writes=[pst])
    def _(e):
        return e.transpose(pst.h[0:ncols, 0:P], colbuf.h[:, 0:ncols], b.ident.h[:, :])

    @prog.step("vector", reads=[pst], writes=[tb])
    def _(e):
        return e.tensor_copy(tb.h[:, :], pst.h[0:ncols, 0:P])

    return prog.step("gpsimd", reads=[tb], writes=[row], dma=True)(
        lambda e: e.dma_start(out=row.h[0:1, 0:ncols * P], in_=tb.h[:, :]))


def emit_l0ffn_wprep(b, ph0, tag):
    prog = b.prog
    b.wqT = [b.sb(tag + f"wqT{ic}", [P, D], BF16, ph0) for ic in range(NCH)]
    b.ws = b.sb(tag + "ws_scale", [1, 1], F32, ph0)
    b.ws_col = b.sb(tag + "ws_col", [P, 1], F32, ph0)
    w_hbm = b.w_hbm["L0_ffn_weight"]

    with contextlib.ExitStack() as ph:
        wrow = [b.sb(tag + f"wrow{oc}", [P, D], F32, ph) for oc in range(NCH)]
        absw = b.sb(tag + "wp_absw", [P, D], F32, ph)
        acc = b.sb(tag + "wp_acc", [P, NCH], F32, ph)
        scal = b.sb(tag + "wp_scal", [P, 1], F32, ph)
        wq_bf = [b.sb(tag + f"wp_wqbf{k}", [P, D], BF16, ph) for k in range(2)]
        ps1 = b.ps(tag + "wp_ps1", [1, 1], F32, ph)
        psb = b.ps(tag + "wp_psb", [P, 1], F32, ph)
        pst = [b.ps(tag + f"wp_pst{k}", [P, D], BF16, ph) for k in range(2)]

        for oc in range(NCH):
            prog.step("sync", writes=[wrow[oc]], dma=True)(
                lambda e, oc=oc: e.dma_start(out=wrow[oc].h[:, :],
                                             in_=w_hbm[oc * P:(oc + 1) * P, :]))
        for oc in range(NCH):
            prog.step("scalar", reads=[wrow[oc]], writes=[absw, acc])(
                lambda e, oc=oc: e.activation(absw.h[:, :], wrow[oc].h[:, :], AF.Abs,
                                              accum_out=acc.h[:, oc:oc + 1]))
        prog.step("vector", reads=[acc], writes=[scal])(
            lambda e: e.tensor_reduce(scal.h[:, :], acc.h[:, :], axis=AX, op=ALU.add))

        @prog.step("tensor", reads=[b.ones_col, scal], writes=[ps1])
        def _(e):
            return e.matmul(ps1.h[:, :], b.ones_col.h[:, :], scal.h[:, :],
                            start=True, stop=True)

        @prog.step("vector", reads=[ps1], writes=[b.ws])
        def _(e):
            return e.tensor_scalar(b.ws.h[:, :], ps1.h[:, :], 1.0 / (D * D), 1e-8,
                                   ALU.mult, ALU.add)

        @prog.step("tensor", reads=[b.ones_row, b.ws], writes=[psb])
        def _(e):
            return e.matmul(psb.h[:, :], b.ones_row.h[:, :], b.ws.h[:, :],
                            start=True, stop=True)

        prog.step("vector", reads=[psb], writes=[b.ws_col])(
            lambda e: e.tensor_copy(b.ws_col.h[:, :], psb.h[:, :]))
        @prog.step("vector", reads=[b.ws_col], writes=[scal])
        def _(e):
            b.vgap(e)
            return e.reciprocal(scal.h[:, :], b.ws_col.h[:, :])

        for oc in range(NCH):
            wqb = wq_bf[oc % 2]
            pp = pst[oc % 2]

            @prog.step("vector", reads=[wrow[oc], scal], writes=[wrow[oc], wqb])
            def _(e, oc=oc, wqb=wqb):
                e.tensor_scalar(wrow[oc].h[:, :], wrow[oc].h[:, :], scal.h[:, 0:1],
                                None, ALU.mult)
                e.tensor_scalar(wrow[oc].h[:, :], wrow[oc].h[:, :], MAGIC, MAGIC,
                                ALU.add, ALU.subtract)
                return e.tensor_scalar(wqb.h[:, :], wrow[oc].h[:, :], -1.0, 1.0,
                                       ALU.max, ALU.min)

            @prog.step("tensor", reads=[wqb, b.identb], writes=[pp])
            def _(e, wqb=wqb, pp=pp):
                for ic in range(NCH):
                    ins = e.transpose(pp.h[:, ic * P:(ic + 1) * P],
                                      wqb.h[:, ic * P:(ic + 1) * P], b.identb.h[:, :])
                return ins

            @prog.step("vector", reads=[pp], writes=list(b.wqT))
            def _(e, oc=oc, pp=pp):
                for ic in range(NCH):
                    ins = e.tensor_copy(b.wqT[ic].h[:, oc * P:(oc + 1) * P],
                                        pp.h[:, ic * P:(ic + 1) * P])
                return ins


def emit_l0ffn(b, tag):
    nc, prog = b.nc, b.prog
    X = b.X
    with contextlib.ExitStack() as ph:
        emit_l0ffn_wprep(b, ph, tag)
        prog.fence()
        negmean = b.sb(tag + "_r0", [1, T], F32, ph)
        rstd = b.sb(tag + "_r1", [1, T], F32, ph)
        arow = b.sb(tag + "_r2", [1, T], F32, ph)
        srow = b.sb(tag + "_r3", [1, T], F32, ph)
        crow = negmean
        emit_ln_stats(b, ph, tag, negmean, rstd, arow, srow)
        mb = b.sb(tag + "_mb", [P, T], F32, ph)
        sbc = b.sb(tag + "_sbc", [P, T], F32, ph)
        tsc = [b.sb(tag + f"_tsc{k}", [P, T], F32, ph) for k in range(2)]
        att = b.sb(tag + "_att", [P, T], F32, ph)
        aq = [b.sb(tag + f"_aq{ic}", [P, T], BF16, ph) for ic in range(NCH)]
        amax_tok = b.sb(tag + "_amx", [P, NTT], F32, ph)
        pbc = b.ps(tag + "_pbc", [P, TS], F32, ph)

        emit_bcast(b, pbc, negmean, mb, tag + "_bc1")

        for ic in range(NCH):
            t = tsc[ic % 2]
            prog.step("vector", reads=[X[ic], mb], writes=[t])(
                lambda e, ic=ic, t=t: e.tensor_tensor(t.h[:, :], X[ic].h[:, :],
                                                      mb.h[:, :], ALU.add))
            if ic == 0:
                @prog.step("vector", reads=[t], writes=[att])
                def _(e, t=t):
                    return e.scalar_tensor_tensor(att.h[:, :], t.h[:, :], -1.0,
                                                  t.h[:, :], ALU.mult, ALU.max)
            else:
                @prog.step("vector", reads=[t, att], writes=[att])
                def _(e, t=t):
                    e.scalar_tensor_tensor(att.h[:, :], t.h[:, :], -1.0,
                                           att.h[:, :], ALU.mult, ALU.max)
                    return e.tensor_tensor(att.h[:, :], att.h[:, :], t.h[:, :],
                                           ALU.max)

        pst2 = [b.ps(tag + f"_pst{k}", [P, TS], F32, ph) for k in range(2)]
        for g in range(NTT // 4):
            pp = pst2[g % 2]

            @prog.step("tensor", reads=[att, b.ident], writes=[pp])
            def _(e, g=g, pp=pp):
                for j in range(4):
                    tc = g * 4 + j
                    ins = e.transpose(pp.h[:, j * P:(j + 1) * P],
                                      att.h[:, tc * P:(tc + 1) * P], b.ident.h[:, :])
                return ins

            @prog.step("vector", reads=[pp], writes=[amax_tok])
            def _(e, g=g, pp=pp):
                for j in range(4):
                    ins = e.tensor_reduce(amax_tok.h[:, g * 4 + j:g * 4 + j + 1],
                                          pp.h[:, j * P:(j + 1) * P], axis=AX,
                                          op=ALU.max)
                return ins

        emit_col_to_row(b, ph, amax_tok, NTT, arow, tag + "_c2r")

        @prog.step("vector", reads=[arow, rstd, b.ws_col],
                   writes=[arow, srow, crow])
        def _(e):
            e.tensor_tensor(arow.h[:, :], arow.h[:, :], rstd.h[:, :], ALU.mult)
            e.tensor_scalar(arow.h[:, :], arow.h[:, :], 1e-8, None, ALU.add)
            e.tensor_scalar(crow.h[:, :], arow.h[:, :], b.ws_col.h[0:1, 0:1],
                            1.0 / 254.0, ALU.mult, ALU.mult)
            b.vgap(e)
            e.reciprocal(arow.h[:, :], arow.h[:, :])
            e.tensor_tensor(srow.h[:, :], arow.h[:, :], rstd.h[:, :], ALU.mult)
            return e.tensor_scalar(srow.h[:, :], srow.h[:, :], 127.0, None, ALU.mult)

        b.tap(negmean)
        b.tap(rstd)
        b.tap(arow)
        b.tap(srow)
        b.tap(crow)
        emit_bcast(b, pbc, srow, sbc, tag + "_bc2")

        for ic in range(NCH):
            t = tsc[ic % 2]
            prog.step("vector", reads=[X[ic], mb], writes=[t])(
                lambda e, ic=ic, t=t: e.tensor_tensor(t.h[:, :], X[ic].h[:, :],
                                                      mb.h[:, :], ALU.add))

            @prog.step("vector", reads=[t, sbc], writes=[t, aq[ic]])
            def _(e, ic=ic, t=t):
                e.tensor_tensor(t.h[:, :], t.h[:, :], sbc.h[:, :], ALU.mult)
                e.tensor_scalar(t.h[:, :], t.h[:, :], -128.0, 127.0, ALU.max, ALU.min)
                return e.tensor_scalar(aq[ic].h[:, :], t.h[:, :], MAGIC, MAGIC,
                                       ALU.add, ALU.subtract)

        emit_bcast(b, pbc, crow, mb, tag + "_bc3")

        psy = [b.ps(tag + f"_psy{k}", [P, TS], F32, ph) for k in range(2)]
        for oc in range(NCH):
            for sl in range(NTS):
                pp = psy[(oc * NTS + sl) % 2]

                @prog.step("tensor", reads=aq + list(b.wqT), writes=[pp])
                def _(e, oc=oc, sl=sl, pp=pp):
                    for ic in range(NCH):
                        ins = e.matmul(pp.h[:, :],
                                       b.wqT[ic].h[:, oc * P:(oc + 1) * P],
                                       aq[ic].h[:, sl * TS:(sl + 1) * TS],
                                       start=(ic == 0), stop=(ic == NCH - 1))
                    return ins

                @prog.step("vector", reads=[pp, mb, X[oc]], writes=[X[oc]])
                def _(e, oc=oc, sl=sl, pp=pp):
                    sli = slice(sl * TS, (sl + 1) * TS)
                    e.tensor_tensor(pp.h[:, :], pp.h[:, :], mb.h[:, sli], ALU.mult)
                    return e.tensor_tensor(X[oc].h[:, sli], X[oc].h[:, sli],
                                           pp.h[:, :], ALU.add)




def emit_load_col(b, ph, hbm, name, ncol=NCH):
    prog = b.prog
    col = b.sb(name, [P, ncol], F32, ph)
    prog.step("gpsimd", writes=[col], dma=True)(
        lambda e: e.dma_start(out=col.h[:, :],
                              in_=bass.AP(hbm, 0, [[1, P], [P, ncol]]),
                              allow_slow_non_contiguous=True))
    return col


def emit_wT_block(b, w_hbm, gc, ldbuf, pst, dst):
    prog = b.prog
    prog.step("gpsimd", writes=[ldbuf], dma=True)(
        lambda e: e.dma_start(out=ldbuf.h[:, :],
                              in_=w_hbm[gc * P:(gc + 1) * P, :]))

    @prog.step("tensor", reads=[ldbuf, b.identb], writes=[pst])
    def _(e):
        for icc in range(NCH):
            ins = e.transpose(pst.h[:, icc * P:(icc + 1) * P],
                              ldbuf.h[:, icc * P:(icc + 1) * P], b.identb.h[:, :])
        return ins

    @prog.step("vector", reads=[pst], writes=[dst])
    def _(e):
        return e.tensor_copy(dst.h[:, :], pst.h[:, :])


LOGEPS = -16.11809565095832


def emit_griffin(b):
    nc, prog = b.nc, b.prog
    X = b.X
    w = b.w_hbm
    with contextlib.ExitStack() as ph:
        xc_bf = [b.sb(f"g_xc{ic}", [P, T], BF16, ph) for ic in range(NCH)]
        cw = b.sb("g_cw", [P, 4 * NCH], F32, ph)
        cb = emit_load_col(b, ph, w["L0_attn_conv_bias"], "g_cb")
        la = emit_load_col(b, ph, w["L0_attn_log_a"], "g_la")
        br = emit_load_col(b, ph, w["L0_attn_Wr_b"], "g_br")
        bi = emit_load_col(b, ph, w["L0_attn_Wi_b"], "g_bi")
        nw = emit_load_col(b, ph, w["L0_attn_norm_w"], "g_nw")
        lnab8 = b.sb("g_lnab8", [P, NCH], F32, ph)

        prog.step("gpsimd", writes=[cw], dma=True)(
            lambda e: e.dma_start(out=cw.h[:, :],
                                  in_=bass.AP(w["L0_attn_conv_weight"], 0,
                                              [[4, P], [512, NCH], [1, 4]]),
                                  allow_slow_non_contiguous=True))

        @prog.step("scalar", reads=[la], writes=[lnab8])
        def _(e):
            e.activation(lnab8.h[:, :], la.h[:, :], AF.Sigmoid)
            return e.activation(lnab8.h[:, :], lnab8.h[:, :], AF.Ln, scale=1.0)

        @prog.step("vector", reads=[lnab8], writes=[lnab8])
        def _(e):
            return e.tensor_scalar(lnab8.h[:, :], lnab8.h[:, :], 8.0, None, ALU.mult)

        with contextlib.ExitStack() as s1:
            negmean = b.sb("g_r0", [1, T], F32, s1)
            rstd = b.sb("g_r1", [1, T], F32, s1)
            ra = b.sb("g_r2", [1, T], F32, s1)
            rb_ = b.sb("g_r3", [1, T], F32, s1)
            emit_ln_stats(b, s1, "gln", negmean, rstd, ra, rb_)
            mb = b.sb("g_mb", [P, T], F32, s1)
            rbt = b.sb("g_rb", [P, T], F32, s1)
            xlnb = [b.sb(f"g_xln{ic}", [P, 3 + T], BF16, s1) for ic in range(NCH)]
            tsc = b.sb("g_tsc", [P, T], F32, s1)
            diag = [[b.sb(f"g_dg{k}_{ic}", [P, P], BF16, s1) for k in range(4)]
                    for ic in range(NCH)]
            pbc = b.ps("g_pbc", [P, TS], F32, s1)
            emit_bcast(b, pbc, negmean, mb, "g_bc1")
            emit_bcast(b, pbc, rstd, rbt, "g_bc2")
            for ic in range(NCH):
                prog.step("gpsimd", writes=[xlnb[ic]])(
                    lambda e, ic=ic: e.memset(xlnb[ic].h[:, 0:3], 0.0))
                prog.step("vector", reads=[X[ic], mb], writes=[tsc])(
                    lambda e, ic=ic: e.tensor_tensor(tsc.h[:, :], X[ic].h[:, :],
                                                     mb.h[:, :], ALU.add))
                prog.step("vector", reads=[tsc, rbt], writes=[xlnb[ic]])(
                    lambda e, ic=ic: e.tensor_tensor(xlnb[ic].h[:, 3:3 + T],
                                                     tsc.h[:, :], rbt.h[:, :],
                                                     ALU.mult))
                for k in range(4):
                    prog.step("vector", reads=[b.ident, cw], writes=[diag[ic][k]])(
                        lambda e, ic=ic, k=k: e.tensor_scalar(
                            diag[ic][k].h[:, :], b.ident.h[:, :],
                            cw.h[:, ic * 4 + k:ic * 4 + k + 1], None, ALU.mult))
            pc = [b.ps(f"g_pc{j}", [P, TS], F32, s1) for j in range(2)]
            for ic in range(NCH):
                for sl in range(NTS):
                    pp = pc[(ic * NTS + sl) % 2]

                    @prog.step("tensor", reads=[xlnb[ic]] + diag[ic], writes=[pp])
                    def _(e, ic=ic, sl=sl, pp=pp):
                        for k in range(4):
                            ins = e.matmul(pp.h[:, :], diag[ic][k].h[:, :],
                                           xlnb[ic].h[:, sl * TS + k:sl * TS + k + TS],
                                           start=(k == 0), stop=(k == 3))
                        return ins

                    @prog.step("vector", reads=[pp, cb], writes=[xc_bf[ic]])
                    def _(e, ic=ic, sl=sl, pp=pp):
                        return e.tensor_scalar(xc_bf[ic].h[:, sl * TS:(sl + 1) * TS],
                                               pp.h[:, :], cb.h[:, ic:ic + 1],
                                               None, ALU.add)

        prog.fence()
        if GRIFFIN_CUT < 2:
            return
        h_bf = [b.sb(f"g_h{ic}", [P, T], BF16, ph) for ic in range(NCH)]
        n80 = b.sb("g_n80", [P, T], F32, ph)
        zt = b.sb("g_zt", [P, T], F32, ph)
        prog.step("gpsimd", writes=[n80])(lambda e: e.memset(n80.h[:, :], -80.0))
        prog.step("gpsimd", writes=[zt])(lambda e: e.memset(zt.h[:, :], 0.0))
        with contextlib.ExitStack() as s2:
            gb = [b.sb(f"g_gb{j}", [P, T], F32, s2) for j in range(6)]
            wrT = [b.sb(f"g_wrT{j}", [P, D], BF16, s2) for j in range(2)]
            wiT = [b.sb(f"g_wiT{j}", [P, D], BF16, s2) for j in range(2)]
            ldb = [b.sb(f"g_ldb{j}", [P, D], BF16, s2) for j in range(2)]
            pg = [b.ps(f"g_pg{j}", [P, TS], F32, s2) for j in range(2)]
            pw = [b.ps(f"g_pw{j}", [P, D], BF16, s2) for j in range(2)]
            for gc in range(NCH if GRIFFIN_CUT >= 4 else 1):
                j = gc % 2
                emit_wT_block(b, w["L0_attn_Wr_w"], gc, ldb[j], pw[j], wrT[j])
                emit_wT_block(b, w["L0_attn_Wi_w"], gc, ldb[j], pw[j], wiT[j])
                for dst, wT, bcol in ((gb[0], wrT[j], br), (gb[1], wiT[j], bi)):
                    for sl in range(NTS):
                        pp = pg[sl % 2]

                        @prog.step("tensor", reads=[wT] + xc_bf, writes=[pp])
                        def _(e, wT=wT, sl=sl, pp=pp):
                            for icc in range(NCH):
                                ins = e.matmul(pp.h[:, :],
                                               wT.h[:, icc * P:(icc + 1) * P],
                                               xc_bf[icc].h[:, sl * TS:(sl + 1) * TS],
                                               start=(icc == 0), stop=(icc == NCH - 1))
                            return ins

                        @prog.step("scalar", reads=[pp, bcol], writes=[dst])
                        def _(e, dst=dst, sl=sl, pp=pp, bcol=bcol, gc=gc):
                            return e.activation(dst.h[:, sl * TS:(sl + 1) * TS],
                                                pp.h[:, :], AF.Sigmoid,
                                                bias=bcol.h[:, gc:gc + 1], scale=1.0)
                if GRIFFIN_CUT < 3:
                    continue
                prog.step("vector", reads=[xc_bf[gc]], writes=[gb[2]])(
                    lambda e, gc=gc: e.tensor_copy(gb[2].h[:, :], xc_bf[gc].h[:, :]))
                prog.step("vector", reads=[gb[0], lnab8], writes=[gb[0]])(
                    lambda e, gc=gc: e.tensor_scalar(gb[0].h[:, :], gb[0].h[:, :],
                                                     lnab8.h[:, gc:gc + 1], None,
                                                     ALU.mult))
                prog.step("scalar", reads=[gb[0]], writes=[gb[3]])(
                    lambda e: e.activation(gb[3].h[:, :], gb[0].h[:, :], AF.Exp))
                prog.step("vector", reads=[gb[0]], writes=[gb[0]])(
                    lambda e: e.tensor_scalar(gb[0].h[:, :], gb[0].h[:, :], LOGEPS,
                                              None, ALU.max))
                prog.step("vector", reads=[gb[0], n80], writes=[gb[4]])(
                    lambda e: e.tensor_tensor_scan(gb[4].h[:, :], gb[0].h[:, :],
                                                   n80.h[:, :], 0.0, ALU.add, ALU.max))
                prog.step("scalar", reads=[gb[3]], writes=[gb[5]])(
                    lambda e: e.activation(gb[5].h[:, :], gb[3].h[:, :], AF.Square))
                prog.step("vector", reads=[gb[5]], writes=[gb[5]])(
                    lambda e: e.tensor_scalar(gb[5].h[:, :], gb[5].h[:, :],
                                              1.0 - 1e-6, None, ALU.min))
                prog.step("scalar", reads=[gb[5]], writes=[gb[5]])(
                    lambda e: e.activation(gb[5].h[:, :], gb[5].h[:, :], AF.Sqrt,
                                           bias=1.0, scale=-1.0))
                prog.step("vector", reads=[gb[5], gb[1]], writes=[gb[5]])(
                    lambda e: e.tensor_tensor(gb[5].h[:, :], gb[5].h[:, :],
                                              gb[1].h[:, :], ALU.mult))
                prog.step("vector", reads=[gb[5], gb[2]], writes=[gb[5]])(
                    lambda e: e.tensor_tensor(gb[5].h[:, :], gb[5].h[:, :],
                                              gb[2].h[:, :], ALU.mult))
                prog.step("scalar", reads=[gb[4]], writes=[gb[1]])(
                    lambda e: e.activation(gb[1].h[:, :], gb[4].h[:, :], AF.Exp,
                                           scale=-1.0))
                prog.step("vector", reads=[gb[5], gb[1]], writes=[gb[5]])(
                    lambda e: e.tensor_tensor(gb[5].h[:, :], gb[5].h[:, :],
                                              gb[1].h[:, :], ALU.mult))
                prog.step("vector", reads=[gb[5], zt], writes=[gb[2]])(
                    lambda e: e.tensor_tensor_scan(gb[2].h[:, :], gb[5].h[:, :],
                                                   zt.h[:, :], 0.0, ALU.add, ALU.add))
                prog.step("scalar", reads=[gb[4]], writes=[gb[1]])(
                    lambda e: e.activation(gb[1].h[:, :], gb[4].h[:, :], AF.Exp))
                prog.step("vector", reads=[gb[1], gb[2]], writes=[h_bf[gc]])(
                    lambda e, gc=gc: e.tensor_tensor(h_bf[gc].h[:, :], gb[1].h[:, :],
                                                     gb[2].h[:, :], ALU.mult))

        if GRIFFIN_CUT < 5:
            return
        prog.fence()
        with contextlib.ExitStack() as s3:
            owT = [b.sb(f"g_owT{j}", [P, D], BF16, s3) for j in range(2)]
            ldo = [b.sb(f"g_ldo{j}", [P, D], BF16, s3) for j in range(2)]
            ysq = [b.sb(f"g_ysq{j}", [P, TS], F32, s3) for j in range(3)]
            y_bf = [b.sb(f"g_y{oc}", [P, T], BF16, s3) for oc in range(NCH)]
            msr = b.sb("g_msr", [1, T], F32, s3)
            py = [b.ps(f"g_py{j}", [P, TS], F32, s3) for j in range(2)]
            pwo = [b.ps(f"g_pwo{j}", [P, D], BF16, s3) for j in range(2)]
            msps = b.ps("g_msps", [1, T], F32, s3)
            pending = []

            def flush_msps():
                if not pending:
                    return
                oc0, sl0, sq0 = pending.pop(0)

                @prog.step("tensor", reads=[sq0, b.ones_col], writes=[msps])
                def _(e, oc0=oc0, sl0=sl0, sq0=sq0):
                    return e.matmul(msps.h[:, sl0 * TS:(sl0 + 1) * TS],
                                    b.ones_col.h[:, :], sq0.h[:, :],
                                    start=(oc0 == 0), stop=(oc0 == NCH - 1))

            for oc in range(NCH):
                j = oc % 2
                emit_wT_block(b, w["L0_attn_out_w"], oc, ldo[j], pwo[j], owT[j])
                for sl in range(NTS):
                    pp = py[sl % 2]

                    @prog.step("tensor", reads=[owT[j]] + h_bf, writes=[pp])
                    def _(e, oc=oc, sl=sl, pp=pp, wT=owT[j]):
                        for icc in range(NCH):
                            ins = e.matmul(pp.h[:, :],
                                           wT.h[:, icc * P:(icc + 1) * P],
                                           h_bf[icc].h[:, sl * TS:(sl + 1) * TS],
                                           start=(icc == 0), stop=(icc == NCH - 1))
                        return ins

                    sq = ysq[(oc * NTS + sl) % 3]
                    sqstep = prog.step("scalar", reads=[pp], writes=[sq])(
                        lambda e, pp=pp, sq=sq: e.activation(sq.h[:, :], pp.h[:, :],
                                                             AF.Square))
                    prog.step("vector", reads=[pp], writes=[y_bf[oc]],
                              after=[sqstep])(
                        lambda e, oc=oc, sl=sl, pp=pp: e.tensor_copy(
                            y_bf[oc].h[:, sl * TS:(sl + 1) * TS], pp.h[:, :]))
                    pending.append((oc, sl, sq))
                    if len(pending) > 1:
                        flush_msps()
            while pending:
                flush_msps()

            if GRIFFIN_CUT < 6 or GRIFFIN_CUT in (51, 52):
                return

            @prog.step("vector", reads=[msps], writes=[msr])
            def _(e):
                e.tensor_scalar(msr.h[:, :], msps.h[:, :], 1.0 / D, 1e-6,
                                ALU.mult, ALU.add)
                b.vgap(e)
                return e.reciprocal(msr.h[:, :], msr.h[:, :])

            prog.step("scalar", reads=[msr], writes=[msr])(
                lambda e: e.activation(msr.h[:, :], msr.h[:, :], AF.Sqrt))

            if GRIFFIN_CUT < 7:
                return
            prb = py[0]
            ptm = py[1]
            for oc in range(NCH):
                for sl in range(NTS):
                    @prog.step("tensor", reads=[b.ones_row, msr], writes=[prb])
                    def _(e, sl=sl):
                        return e.matmul(prb.h[:, :], b.ones_row.h[:, :],
                                        msr.h[0:1, sl * TS:(sl + 1) * TS],
                                        start=True, stop=True)

                    @prog.step("vector", reads=[prb, y_bf[oc]], writes=[ptm])
                    def _(e, oc=oc, sl=sl):
                        return e.tensor_tensor(ptm.h[:, :],
                                               y_bf[oc].h[:, sl * TS:(sl + 1) * TS],
                                               prb.h[:, :], ALU.mult)

                    @prog.step("vector", reads=[ptm, nw, X[oc]], writes=[X[oc]])
                    def _(e, oc=oc, sl=sl):
                        sli = slice(sl * TS, (sl + 1) * TS)
                        return e.scalar_tensor_tensor(X[oc].h[:, sli], ptm.h[:, :],
                                                      nw.h[:, oc:oc + 1],
                                                      X[oc].h[:, sli],
                                                      ALU.mult, ALU.add)




ATTN_SCALE = float(1.0 / (np.sqrt(np.float32(1024.0)) + 1e-8))
DEC_C = 1e-4
MH_SLACK = 2.2688
QQ = 512
NQQ = T // QQ


def emit_ln_bcast(b, ph, tag, mb, rb):
    nc, prog = b.nc, b.prog
    X = b.X
    with contextlib.ExitStack() as st:
        sq = [b.sb(tag + f"_sq{j}", [P, TS], F32, st) for j in range(2)]
        ps_s = b.ps(tag + "_pss", [1, T], F32, st)
        stq = contextlib.ExitStack()
        ps_q = b.ps(tag + "_psq", [1, T], F32, stq)
        negrow = b.sb(tag + "_negrow", [1, T], F32, st)
        for ic in range(NCH):
            for sl in range(NTS):
                sqb = sq[sl % 2]
                prog.step("scalar", reads=[X[ic]], writes=[sqb])(
                    lambda e, ic=ic, sl=sl, sqb=sqb: e.activation(
                        sqb.h[:, :], X[ic].h[:, sl * TS:(sl + 1) * TS], AF.Square))

                @prog.step("tensor", reads=[b.ones_col, X[ic], sqb],
                           writes=[ps_s, ps_q])
                def _(e, ic=ic, sl=sl, sqb=sqb):
                    e.matmul(ps_s.h[:, sl * TS:(sl + 1) * TS], b.ones_col.h[:, :],
                             X[ic].h[:, sl * TS:(sl + 1) * TS],
                             start=(ic == 0), stop=(ic == NCH - 1))
                    return e.matmul(ps_q.h[:, sl * TS:(sl + 1) * TS],
                                    b.ones_col.h[:, :], sqb.h[:, :],
                                    start=(ic == 0), stop=(ic == NCH - 1))

        @prog.step("vector", reads=[ps_s, ps_q], writes=[mb, rb, negrow])
        def _(e):
            e.tensor_scalar(negrow.h[:, :], ps_s.h[:, :], -1.0 / D, None, ALU.mult)
            e.tensor_scalar(rb.h[0:1, 0:T], ps_q.h[:, :], 1.0 / D, None, ALU.mult)
            e.tensor_tensor(mb.h[0:1, 0:T], negrow.h[:, :], negrow.h[:, :], ALU.mult)
            e.tensor_tensor(rb.h[0:1, 0:T], rb.h[0:1, 0:T], mb.h[0:1, 0:T],
                            ALU.subtract)
            e.tensor_scalar(rb.h[0:1, 0:T], rb.h[0:1, 0:T], 1e-5, None, ALU.add)
            b.vgap(e)
            return e.reciprocal(mb.h[0:1, 0:T], rb.h[0:1, 0:T])

        prog.step("scalar", reads=[mb], writes=[mb])(
            lambda e: e.activation(mb.h[0:1, 0:T], mb.h[0:1, 0:T], AF.Sqrt))

        stq.close()
        prog.fence()
        pbc = b.ps(tag + "_pbc", [P, TS], F32, st)
        for sl in range(NTS):
            @prog.step("tensor", reads=[b.ones_row, mb], writes=[pbc])
            def _(e, sl=sl):
                return e.matmul(pbc.h[:, :], b.ones_row.h[:, :],
                                mb.h[0:1, sl * TS:(sl + 1) * TS],
                                start=True, stop=True)

            prog.step("vector", reads=[pbc], writes=[rb])(
                lambda e, sl=sl: e.tensor_copy(rb.h[:, sl * TS:(sl + 1) * TS],
                                               pbc.h[:, :]))
        for sl in range(NTS):
            @prog.step("tensor", reads=[b.ones_row, negrow], writes=[pbc])
            def _(e, sl=sl):
                return e.matmul(pbc.h[:, :], b.ones_row.h[:, :],
                                negrow.h[0:1, sl * TS:(sl + 1) * TS],
                                start=True, stop=True)

            prog.step("vector", reads=[pbc], writes=[mb])(
                lambda e, sl=sl: e.tensor_copy(mb.h[:, sl * TS:(sl + 1) * TS],
                                               pbc.h[:, :]))
    prog.fence()


def emit_l1attn(b):
    nc, prog = b.nc, b.prog
    X = b.X
    w = b.w_hbm
    c_hbm = b.c_hbm
    xlns = b.xln_hbm
    with contextlib.ExitStack() as ph:
        qm = [b.sb(f"a_qm{c}", [P, T], BF16, ph) for c in range(NCH)]
        kp = [b.sb(f"a_kp{c}", [P, T], BF16, ph) for c in range(NCH)]
        vt = [b.sb(f"a_vt{i}", [P, 65 * 16], BF16, ph) for i in range(NTT)]
        e2m = b.sb("a_e2m", [P, T], BF16, ph)
        e2p = b.sb("a_e2p", [P, T], BF16, ph)
        mh = b.sb("a_mh", [P, NCH], F32, ph)

        with contextlib.ExitStack() as s1:
            comp = [b.sb(f"a_cp{ic}", [P, 256], BF16, s1) for ic in range(NCH)]
            with contextlib.ExitStack() as s0:
                wtmp = b.sb("a_wt1", [P, D], BF16, s0)
                wtmp2 = b.sb("a_wt2", [P, D], BF16, s0)
                wkv = [b.sb(f"a_wkv{j}", [P, D], BF16, s0) for j in range(2)]
                wkvT = [b.sb(f"a_wkvT{oc}", [P, D], BF16, s0) for oc in range(NCH)]
                wdn = [b.sb(f"a_wdn{oc}", [P, 256], BF16, s0) for oc in range(NCH)]
                pt = [b.ps(f"a_pt{j}", [P, D], BF16, s0) for j in range(2)]
                pcp = b.ps("a_pcp", [P, 256], F32, s0)
                for ic in range(NCH):
                    prog.step("gpsimd", writes=[wtmp], dma=True)(
                        lambda e, ic=ic: e.dma_start(out=wtmp.h[:, :],
                                                     in_=w["L1_attn_Wk"][ic * P:(ic + 1) * P, :]))
                    prog.step("gpsimd", writes=[wtmp2], dma=True)(
                        lambda e, ic=ic: e.dma_start(out=wtmp2.h[:, :],
                                                     in_=w["L1_attn_Wv"][ic * P:(ic + 1) * P, :]))
                    kv = wkv[ic % 2]
                    prog.step("vector", reads=[wtmp, wtmp2], writes=[kv])(
                        lambda e, kv=kv: e.tensor_tensor(kv.h[:, :], wtmp.h[:, :],
                                                         wtmp2.h[:, :], ALU.add))
                    prog.step("gpsimd", writes=[wdn[ic]], dma=True)(
                        lambda e, ic=ic: e.dma_start(out=wdn[ic].h[:, :],
                                                     in_=w["L1_attn_W_down_kv"][ic * P:(ic + 1) * P, :]))
                    pp = pt[ic % 2]

                    @prog.step("tensor", reads=[kv, b.identb], writes=[pp])
                    def _(e, kv=kv, pp=pp):
                        for oc in range(NCH):
                            ins = e.transpose(pp.h[:, oc * P:(oc + 1) * P],
                                              kv.h[:, oc * P:(oc + 1) * P],
                                              b.identb.h[:, :])
                        return ins

                    @prog.step("vector", reads=[pp], writes=wkvT)
                    def _(e, ic=ic, pp=pp):
                        for oc in range(NCH):
                            ins = e.tensor_copy(wkvT[oc].h[:, ic * P:(ic + 1) * P],
                                                pp.h[:, oc * P:(oc + 1) * P])
                        return ins
                for ic in range(NCH):
                    @prog.step("tensor", reads=wkvT + wdn, writes=[pcp])
                    def _(e, ic=ic):
                        for oc in range(NCH):
                            ins = e.matmul(pcp.h[:, :],
                                           wkvT[oc].h[:, ic * P:(ic + 1) * P],
                                           wdn[oc].h[:, :],
                                           start=(oc == 0), stop=(oc == NCH - 1))
                        return ins

                    prog.step("vector", reads=[pcp], writes=[comp[ic]])(
                        lambda e, ic=ic: e.tensor_copy(comp[ic].h[:, :], pcp.h[:, :]))
            prog.fence()

            with contextlib.ExitStack() as s1c:
                mbx = b.sb("a_mb", [P, T], F32, s1c)
                rbt = b.sb("a_rb", [P, T], F32, s1c)
                emit_ln_bcast(b, s1c, "aln", mbx, rbt)
                sg = b.sb("a_stg0", [P, T], BF16, s1c)
                tscp = b.ps("a_tscp", [P, TS], F32, s1c)
                for ic in range(NCH):
                    for sl in range(NTS):
                        prog.step("vector", reads=[X[ic], mbx], writes=[tscp])(
                            lambda e, ic=ic, sl=sl: e.tensor_tensor(
                                tscp.h[:, :], X[ic].h[:, sl * TS:(sl + 1) * TS],
                                mbx.h[:, sl * TS:(sl + 1) * TS], ALU.add))
                        prog.step("vector", reads=[tscp, rbt], writes=[sg])(
                            lambda e, ic=ic, sl=sl: e.tensor_tensor(
                                sg.h[:, sl * TS:(sl + 1) * TS], tscp.h[:, :],
                                rbt.h[:, sl * TS:(sl + 1) * TS], ALU.mult))
                    prog.step("sync", reads=[sg], dma=True)(
                        lambda e, ic=ic: e.dma_start(
                            out=xlns[ic * P:(ic + 1) * P, :], in_=sg.h[:, :]))
            prog.fence()
            latT = [b.sb(f"a_lat{j}", [P, T], BF16, s1) for j in range(2)]
            etmp = b.sb("a_etmp", [P, T], BF16, s1)
            prog.step("gpsimd", writes=[etmp], dma=True)(
                lambda e: e.dma_start(out=etmp.h[:, :], in_=c_hbm["edecm"][:, :]))
            prog.step("vector", reads=[etmp], writes=[e2m])(
                lambda e: e.tensor_tensor(e2m.h[:, :], etmp.h[:, :],
                                          etmp.h[:, :], ALU.mult))
            prog.step("gpsimd", writes=[etmp], dma=True)(
                lambda e: e.dma_start(out=etmp.h[:, :], in_=c_hbm["edecp"][:, :]))
            prog.step("vector", reads=[etmp], writes=[e2p])(
                lambda e: e.tensor_tensor(e2p.h[:, :], etmp.h[:, :],
                                          etmp.h[:, :], ALU.mult))

            esl = [b.sb(f"a_esl{j}", [P, TS], BF16, s1) for j in range(2)]
            s1d = contextlib.ExitStack()
            xcol = [b.sb(f"a_xc{ic}", [P, TS], BF16, s1d) for ic in range(NCH)]
            wch = [b.sb(f"a_wch{j}", [P, D], BF16, s1d) for j in range(2)]
            pj = [b.ps(f"a_pj{oc}", [P, TS], F32, s1d) for oc in range(NCH)]

            def stream_proj(w_name, nic, out_fn):
                for sl in range(NTS):
                    for ic in range(nic):
                        wc = wch[ic % 2]
                        prog.step("gpsimd", writes=[wc], dma=True)(
                            lambda e, ic=ic, wc=wc: e.dma_start(
                                out=wc.h[:, :],
                                in_=w[w_name][ic * P:(ic + 1) * P, :]))
                        prog.step("sync", writes=[xcol[ic]], dma=True)(
                            lambda e, ic=ic, sl=sl: e.dma_start(
                                out=xcol[ic].h[:, :],
                                in_=xlns[ic * P:(ic + 1) * P, sl * TS:(sl + 1) * TS]))

                        @prog.step("tensor", reads=[wc, xcol[ic]], writes=pj)
                        def _(e, ic=ic, wc=wc, nic=nic):
                            for oc in range(NCH):
                                ins = e.matmul(pj[oc].h[:, :],
                                               wc.h[:, oc * P:(oc + 1) * P],
                                               xcol[ic].h[:, :],
                                               start=(ic == 0), stop=(ic == nic - 1))
                            return ins
                    for oc in range(NCH):
                        out_fn(oc, sl, pj[oc])

            def evac_q(oc, sl, pp):
                ee = esl[sl % 2]
                if oc == 0:
                    prog.step("gpsimd", writes=[ee], dma=True)(
                        lambda e, sl=sl, ee=ee: e.dma_start(
                            out=ee.h[:, :],
                            in_=c_hbm["edecm"][:, sl * TS:(sl + 1) * TS]))
                prog.step("vector", reads=[pp, ee], writes=[qm[oc]])(
                    lambda e, oc=oc, sl=sl, pp=pp, ee=ee: e.tensor_tensor(
                        qm[oc].h[:, sl * TS:(sl + 1) * TS], pp.h[:, :],
                        ee.h[:, :], ALU.mult))

            stream_proj("L1_attn_Wq", NCH, evac_q)

            for sl in range(NTS):
                for ic in range(NCH):
                    prog.step("sync", writes=[xcol[ic]], dma=True)(
                        lambda e, ic=ic, sl=sl: e.dma_start(
                            out=xcol[ic].h[:, :],
                            in_=xlns[ic * P:(ic + 1) * P, sl * TS:(sl + 1) * TS]))

                    @prog.step("tensor", reads=[comp[ic], xcol[ic]],
                               writes=[pj[0], pj[1]])
                    def _(e, ic=ic):
                        for lc in range(2):
                            ins = e.matmul(pj[lc].h[:, :],
                                           comp[ic].h[:, lc * P:(lc + 1) * P],
                                           xcol[ic].h[:, :],
                                           start=(ic == 0), stop=(ic == NCH - 1))
                        return ins
                for lc in range(2):
                    prog.step("vector", reads=[pj[lc]], writes=[latT[lc]])(
                        lambda e, lc=lc, sl=sl: e.tensor_copy(
                            latT[lc].h[:, sl * TS:(sl + 1) * TS], pj[lc].h[:, :]))
            s1d.close()
            prog.fence()

            pk = [b.ps(f"a_pk{j}", [P, TS], F32, s1) for j in range(2)]
            wu = [b.sb(f"a_wu{j}", [P, D], BF16, s1) for j in range(2)]
            wvv = [b.sb(f"a_wv{j}", [P, D], BF16, s1) for j in range(2)]
            for j in range(2):
                prog.step("gpsimd", writes=[wu[j]], dma=True)(
                    lambda e, j=j: e.dma_start(out=wu[j].h[:, :],
                                               in_=w["L1_attn_W_up_k"][j * P:(j + 1) * P, :]))
                prog.step("gpsimd", writes=[wvv[j]], dma=True)(
                    lambda e, j=j: e.dma_start(out=wvv[j].h[:, :],
                                               in_=w["L1_attn_W_up_v"][j * P:(j + 1) * P, :]))
            for oc in range(NCH):
                for sl in range(NTS):
                    pp = pk[(oc * NTS + sl) % 2]

                    @prog.step("tensor", reads=wu + latT, writes=[pp])
                    def _(e, oc=oc, sl=sl, pp=pp):
                        for j in range(2):
                            ins = e.matmul(pp.h[:, :],
                                           wu[j].h[:, oc * P:(oc + 1) * P],
                                           latT[j].h[:, sl * TS:(sl + 1) * TS],
                                           start=(j == 0), stop=(j == 1))
                        return ins

                    ee = esl[sl % 2]
                    prog.step("gpsimd", writes=[ee], dma=True)(
                        lambda e, sl=sl, ee=ee: e.dma_start(
                            out=ee.h[:, :],
                            in_=c_hbm["edecp"][:, sl * TS:(sl + 1) * TS]))
                    prog.step("vector", reads=[pp, ee], writes=[kp[oc]])(
                        lambda e, oc=oc, sl=sl, pp=pp, ee=ee: e.tensor_tensor(
                            kp[oc].h[:, sl * TS:(sl + 1) * TS], pp.h[:, :],
                            ee.h[:, :], ALU.mult))
            pv = b.ps("a_pv", [P, D], F32, s1)
            for tt in range(NTT):
                prog.step("gpsimd", writes=[vt[tt]])(
                    lambda e, tt=tt: e.memset(
                        bass.AP(vt[tt].h, 64, [[65 * 16, P], [65, 16], [1, 1]]), 1.0))

                @prog.step("tensor", reads=wvv + latT, writes=[pv])
                def _(e, tt=tt):
                    for half in range(2):
                        for j in range(2):
                            ins = e.matmul(pv.h[:, half * TS:(half + 1) * TS],
                                           latT[j].h[:, tt * P:(tt + 1) * P],
                                           wvv[j].h[:, half * TS:(half + 1) * TS],
                                           start=(j == 0), stop=(j == 1))
                    return ins

                @prog.step("vector", reads=[pv], writes=[vt[tt]])
                def _(e, tt=tt):
                    return e.tensor_copy(
                        bass.AP(vt[tt].h, 0, [[65 * 16, P], [65, 16], [1, 64]]),
                        bass.AP(pv.h, 0, [[D, P], [64, 16], [1, 64]]))
            pn = b.ps("a_pn", [1, TS], F32, s1)
            nsq2 = [b.sb(f"a_nq{j}", [P, TS], F32, s1) for j in range(2)]
            bqk = b.sb("a_bqk", [1, 2 * NCH], F32, s1)
            bsl = b.sb("a_bsl", [1, NTS], F32, s1)
            mrow = b.sb("a_mrow", [1, NCH], F32, s1)
            for kind, tiles in ((0, qm), (1, kp)):
                for c in range(NCH):
                    for sl in range(NTS):
                        nq = nsq2[sl % 2]
                        prog.step("scalar", reads=[tiles[c]], writes=[nq])(
                            lambda e, c=c, sl=sl, nq=nq, tiles=tiles: e.activation(
                                nq.h[:, :],
                                tiles[c].h[:, sl * TS:(sl + 1) * TS], AF.Square))

                        @prog.step("tensor", reads=[nq, b.ones_col], writes=[pn])
                        def _(e, nq=nq):
                            return e.matmul(pn.h[:, :], b.ones_col.h[:, :],
                                            nq.h[:, :], start=True, stop=True)

                        prog.step("vector", reads=[pn], writes=[bsl])(
                            lambda e, sl=sl: e.tensor_reduce(
                                bsl.h[0:1, sl:sl + 1], pn.h[:, :],
                                axis=AX, op=ALU.max))

                    prog.step("vector", reads=[bsl], writes=[bqk])(
                        lambda e, kind=kind, c=c: e.tensor_reduce(
                            bqk.h[0:1, kind * NCH + c:kind * NCH + c + 1],
                            bsl.h[:, :], axis=AX, op=ALU.max))

            @prog.step("vector", reads=[bqk], writes=[mrow])
            def _(e):
                return e.tensor_tensor(mrow.h[:, :], bqk.h[0:1, 0:NCH],
                                       bqk.h[0:1, NCH:2 * NCH], ALU.mult)

            prog.step("scalar", reads=[mrow], writes=[mrow])(
                lambda e: e.activation(mrow.h[:, :], mrow.h[:, :], AF.Sqrt))

            @prog.step("vector", reads=[mrow], writes=[mrow])
            def _(e):
                return e.tensor_scalar(mrow.h[:, :], mrow.h[:, :],
                                       -ATTN_SCALE * MH_SLACK, 40.0,
                                       ALU.mult, ALU.add)

            pmh = b.ps("a_pmh", [P, NCH], F32, s1)

            @prog.step("tensor", reads=[b.ones_row, mrow], writes=[pmh])
            def _(e):
                return e.matmul(pmh.h[:, :], b.ones_row.h[:, :], mrow.h[:, :],
                                start=True, stop=True)

            prog.step("vector", reads=[pmh], writes=[mh])(
                lambda e: e.tensor_copy(mh.h[:, :], pmh.h[:, :]))
        prog.fence()

        if ATTN_CUT < 2:
            return
        with contextlib.ExitStack() as s2:
            corr = b.sb("a_corr", [P, P], F32, s2)
            prog.step("sync", writes=[corr], dma=True)(
                lambda e: e.dma_start(out=corr.h[:, :], in_=c_hbm["corr"][:, :]))
            kst = b.sb("a_kst", [64, T + QQ], BF16, s2)
            qph = b.sb("a_qph", [64, QQ], BF16, s2)
            qlo = b.sb("a_qlo", [64, QQ], BF16, s2)
            expb = [b.sb(f"a_exp{j}", [P, QQ], BF16, s2) for j in range(2)]
            ao = [b.sb(f"a_ao{c}", [P, QQ], BF16, s2) for c in range(NCH)]
            rbc = b.sb("a_rbc", [64, QQ], F32, s2)
            wo = [b.sb(f"a_wo{ic}", [P, D], BF16, s2) for ic in range(NCH)]
            psc = [b.ps(f"a_psc{j}", [P, QQ], F32, s2) for j in range(2)]
            pav = b.ps("a_pav", [65, QQ], F32, s2)
            pxy = [b.ps(f"a_pxy{j}", [P, QQ], F32, s2) for j in range(2)]
            prbc = b.ps("a_prbc", [64, QQ], F32, s2)
            for ic in range(NCH):
                prog.step("gpsimd", writes=[wo[ic]], dma=True)(
                    lambda e, ic=ic: e.dma_start(out=wo[ic].h[:, :],
                                                 in_=w["L1_attn_Wo"][ic * P:(ic + 1) * P, :]))
            for qq in range(NQQ):
                q0 = qq * QQ
                for h in range(16):
                    c = h // 2
                    base = (h % 2) * 64
                    prog.step("vector", reads=[kp[c], e2m], writes=[kst])(
                        lambda e, c=c, base=base, q0=q0: e.tensor_tensor(
                            kst.h[:, q0 + QQ:T + QQ],
                            kp[c].h[base:base + 64, q0:T],
                            e2m.h[base:base + 64, q0:T], ALU.mult))
                    prog.step("vector", reads=[qm[c], e2p], writes=[qph])(
                        lambda e, c=c, base=base, q0=q0: e.tensor_tensor(
                            qph.h[:, :], qm[c].h[base:base + 64, q0:q0 + QQ],
                            e2p.h[base:base + 64, q0:q0 + QQ], ALU.mult))
                    if base:
                        prog.step("vector", reads=[kp[c]], writes=[kst])(
                            lambda e, c=c, q0=q0: e.tensor_copy(
                                kst.h[:, 0:q0 + QQ], kp[c].h[64:128, 0:q0 + QQ]))
                        prog.step("vector", reads=[qm[c]], writes=[qlo])(
                            lambda e, c=c, q0=q0: e.tensor_copy(
                                qlo.h[:, :], qm[c].h[64:128, q0:q0 + QQ]))
                        kl, ql, qc0 = kst, qlo, 0
                    else:
                        kl, ql, qc0 = kp[c], qm[c], q0
                    if ATTN_SUB < 2:
                        continue
                    for kc in range(NTT):
                        K0 = kc * P
                        pp = psc[kc % 2]
                        lo = K0 + P <= q0
                        up = K0 >= q0 + QQ

                        @prog.step("tensor", reads=[kl, ql, kst, qph],
                                   writes=[pp])
                        def _(e, kl=kl, ql=ql, qc0=qc0, q0=q0, K0=K0, pp=pp,
                              lo=lo, up=up):
                            ins = None
                            if not up:
                                ins = e.matmul(pp.h[:, :],
                                               kl.h[0:64, K0:K0 + P],
                                               ql.h[0:64, qc0:qc0 + QQ],
                                               start=True, stop=True)
                            if not lo:
                                hi = QQ if up else min(QQ, K0 + P - q0)
                                if hi > 0:
                                    ins = e.matmul(pp.h[:, 0:hi],
                                                   kst.h[:, K0 + QQ:K0 + QQ + P],
                                                   qph.h[:, 0:hi],
                                                   start=True, stop=True)
                            return ins

                        if (not lo) and (not up) and ATTN_SUB >= 3:
                            c0 = max(0, K0 - q0)
                            c1 = min(QQ, K0 - q0 + P)
                            j0 = c0 - (K0 - q0)
                            if c1 > c0:
                                @prog.step("vector", reads=[pp, corr], writes=[pp])
                                def _(e, pp=pp, c0=c0, c1=c1, j0=j0):
                                    return e.tensor_tensor(
                                        pp.h[:, c0:c1], pp.h[:, c0:c1],
                                        corr.h[:, j0:j0 + (c1 - c0)], ALU.mult)

                        eb = expb[kc % 2]
                        if ATTN_SUB >= 4:
                            prog.step("scalar", reads=[pp, mh], writes=[eb])(
                                lambda e, pp=pp, eb=eb, c=c: e.activation(
                                    eb.h[:, :], pp.h[:, :], AF.Exp,
                                    bias=mh.h[:, c:c + 1], scale=ATTN_SCALE))
                        elif ATTN_SUB >= 2:
                            prog.step("vector", reads=[pp], writes=[eb])(
                                lambda e, pp=pp, eb=eb: e.tensor_copy(
                                    eb.h[:, :], pp.h[:, :]))

                        if ATTN_CUT >= 3:
                            @prog.step("tensor", reads=[eb, vt[kc]], writes=[pav])
                            def _(e, eb=eb, kc=kc, h=h):
                                return e.matmul(pav.h[:, :],
                                                vt[kc].h[:, h * 65:(h + 1) * 65],
                                                eb.h[:, :], start=(kc == 0),
                                                stop=(kc == NTT - 1))

                    if ATTN_CUT < 4:
                        continue

                    @prog.step("vector", reads=[pav], writes=[rbc])
                    def _(e):
                        b.vgap(e)
                        return e.reciprocal(rbc.h[0:1, :], pav.h[64:65, :])

                    @prog.step("tensor", reads=[b.ones_row, rbc], writes=[prbc])
                    def _(e):
                        return e.matmul(prbc.h[:, :], b.ones_row.h[0:1, 0:64],
                                        rbc.h[0:1, :], start=True, stop=True)

                    prog.step("vector", reads=[prbc], writes=[rbc])(
                        lambda e: e.tensor_copy(rbc.h[:, :], prbc.h[:, :]))

                    @prog.step("vector", reads=[pav, rbc], writes=[ao[c]])
                    def _(e, c=c, base=base):
                        return e.tensor_tensor(ao[c].h[base:base + 64, :],
                                               pav.h[0:64, :], rbc.h[:, :], ALU.mult)
                for oc in range(NCH if ATTN_CUT >= 4 else 0):
                    pp = pxy[oc % 2]

                    @prog.step("tensor", reads=wo + ao, writes=[pp])
                    def _(e, oc=oc, pp=pp):
                        for ic in range(NCH):
                            ins = e.matmul(pp.h[:, :],
                                           wo[ic].h[:, oc * P:(oc + 1) * P],
                                           ao[ic].h[:, :],
                                           start=(ic == 0), stop=(ic == NCH - 1))
                        return ins

                    @prog.step("vector", reads=[pp, X[oc]], writes=[X[oc]])
                    def _(e, oc=oc, pp=pp, q0=q0):
                        return e.tensor_tensor(X[oc].h[:, q0:q0 + QQ],
                                               X[oc].h[:, q0:q0 + QQ],
                                               pp.h[:, :], ALU.add)



C_CAP = 2560
NSLOT = 8 * C_CAP
NS_TOT = 20608
NTILE = C_CAP // P
BIS_IT = 50
MOE_CUT = 99


def emit_digits(b, t1, t2, src, dst, scale, nrow):
    prog = b.prog

    @prog.step("vector", reads=[src], writes=[t1])
    def _(e):
        return e.tensor_scalar(t1.h[:, :], src.h[0:nrow, 0:1], 1.0 / scale,
                               -0.49609375, ALU.mult, ALU.add)

    prog.step("vector", reads=[t1], writes=[t1])(
        lambda e: e.tensor_scalar(t1.h[:, :], t1.h[:, :], MAGIC, -MAGIC,
                                  ALU.add, ALU.add))
    prog.step("vector", reads=[t1, src], writes=[t2])(
        lambda e: e.scalar_tensor_tensor(t2.h[:, :], t1.h[:, :], -float(scale),
                                         src.h[0:nrow, 0:1], ALU.mult, ALU.add))
    prog.step("vector", reads=[t1], writes=[dst])(
        lambda e: e.tensor_copy(dst.h[0:nrow, 0:1], t1.h[:, :]))
    prog.step("vector", reads=[t2], writes=[dst])(
        lambda e: e.tensor_copy(dst.h[0:nrow, 1:2], t2.h[:, :]))


def emit_l1ffn(b, tag):
    nc, prog = b.nc, b.prog
    X = b.X
    w = b.w_hbm
    hb = b.moe_hbm[tag]
    xnpub = Buf(hb["xnpub"])
    xng = Buf(hb["xng"])
    wtpub = Buf(hb["wtpub"])
    wtg = Buf(hb["wtg"])
    stin = Buf(hb["stin"])
    stout = Buf(hb["stout"])
    pubout = Buf(hb["pubout"])
    outg = Buf(hb["outg"])

    with contextlib.ExitStack() as pf:
        i1a = b.sb(tag + "_i1a", [P, 16], F32, pf)
        i2a = b.sb(tag + "_i2a", [P, 16], F32, pf)
        w1a = b.sb(tag + "_w1a", [P, 16], F32, pf)
        w2a = b.sb(tag + "_w2a", [P, 16], F32, pf)
        w1k = b.sb(tag + "_w1k", [P, 16], F32, pf)
        w2k = b.sb(tag + "_w2k", [P, 16], F32, pf)
        sx1 = b.sb(tag + "_sx1", [P, 16], F32, pf)
        sx2 = b.sb(tag + "_sx2", [P, 16], F32, pf)
        iota8 = b.sb(tag + "_io8", [P, 8], F32, pf)
        im99 = b.sb(tag + "_im99", [P, 8], F32, pf)
        piota = b.sb(tag + "_pio", [P, 1], F32, pf)
        Wt = b.sb(tag + "_wt", [8, T], F32, pf)
        coreid = b.sb(tag + "_cid", [P, 1], F32, pf)
        cummask = b.sb(tag + "_cmk", [64, 8], BF16, pf)
        sel64 = b.sb(tag + "_s64", [64, 8], BF16, pf)
        selcum = b.sb(tag + "_scm", [64, 64], BF16, pf)
        lo = b.sb(tag + "_lo", [8, 1], F32, pf)
        hi = b.sb(tag + "_hi", [8, 1], F32, pf)
        offs8 = b.sb(tag + "_of8", [8, 1], F32, pf)
        toffs = b.sb(tag + "_tof", [8, 1], F32, pf)
        Rbud = b.sb(tag + "_rbd", [8, 1], F32, pf)

        prog.step("gpsimd", writes=[iota8])(
            lambda e: e.iota(iota8.h[:, :], [[1, 8]], channel_multiplier=0,
                             allow_small_or_imprecise_dtypes=True))
        prog.step("gpsimd", writes=[piota])(
            lambda e: e.iota(piota.h[:, :], [[0, 1]], channel_multiplier=1,
                             allow_small_or_imprecise_dtypes=True))
        prog.step("vector", reads=[iota8], writes=[im99])(
            lambda e: e.tensor_scalar(im99.h[:, :], iota8.h[:, :], -99.0,
                                      None, ALU.add))
        prog.step("sync", writes=[coreid], dma=True)(
            lambda e: e.dma_start(out=coreid.h[:, :], in_=b.c_hbm["coreid"][:, :]))
        prog.step("sync", writes=[cummask], dma=True)(
            lambda e: e.dma_start(out=cummask.h[:, :], in_=b.c_hbm["cummask"][:, :]))
        prog.step("sync", writes=[sel64], dma=True)(
            lambda e: e.dma_start(out=sel64.h[:, :], in_=b.c_hbm["sel64"][:, :]))
        prog.step("sync", writes=[selcum], dma=True)(
            lambda e: e.dma_start(out=selcum.h[:, :], in_=b.c_hbm["selcum"][:, :]))

        with contextlib.ExitStack() as p0:
            xnF = [b.sb(tag + f"_xn{c}", [P, T], BF16, p0) for c in range(NCH)]
            mbx = b.sb(tag + "_mb", [P, T], F32, p0)
            rbt = b.sb(tag + "_rb", [P, T], F32, p0)
            emit_ln_bcast(b, p0, tag + "ln", mbx, rbt)
            tscp = b.ps(tag + "_tsc", [P, TS], F32, p0)
            for ic in range(NCH):
                for sl in range(NTS):
                    prog.step("vector", reads=[X[ic], mbx], writes=[tscp])(
                        lambda e, ic=ic, sl=sl: e.tensor_tensor(
                            tscp.h[:, :], X[ic].h[:, sl * TS:(sl + 1) * TS],
                            mbx.h[:, sl * TS:(sl + 1) * TS], ALU.add))
                    prog.step("vector", reads=[tscp, rbt], writes=[xnF[ic]])(
                        lambda e, ic=ic, sl=sl: e.tensor_tensor(
                            xnF[ic].h[:, sl * TS:(sl + 1) * TS], tscp.h[:, :],
                            rbt.h[:, sl * TS:(sl + 1) * TS], ALU.mult))
            with contextlib.ExitStack() as p0a:
                ptp = [b.ps(tag + f"_ptp{j}", [P, D], BF16, p0a)
                       for j in range(2)]
                tokrow = [b.sb(tag + f"_tkr{j}", [P, D], BF16, p0a)
                          for j in range(2)]
                for tt in range(NTT):
                    pp = ptp[tt % 2]
                    tr = tokrow[tt % 2]

                    @prog.step("tensor", reads=xnF + [b.identb], writes=[pp])
                    def _(e, tt=tt, pp=pp):
                        for fc in range(NCH):
                            ins = e.transpose(pp.h[:, fc * P:(fc + 1) * P],
                                              xnF[fc].h[:, tt * P:(tt + 1) * P],
                                              b.identb.h[:, :])
                        return ins

                    prog.step("vector", reads=[pp], writes=[tr])(
                        lambda e, pp=pp, tr=tr: e.tensor_copy(tr.h[:, :],
                                                              pp.h[:, :]))
                    prog.step("sync", reads=[tr], writes=[xnpub], dma=True)(
                        lambda e, tr=tr, tt=tt: e.dma_start(
                            out=hb["xnpub"][tt * P:(tt + 1) * P, :],
                            in_=tr.h[:, :]))
            prog.fence()
            rw8 = b.sb(tag + "_rw8", [8, D], BF16, p0)
            rwT = b.sb(tag + "_rwT", [P, 64], BF16, p0)
            prw = b.ps(tag + "_prw", [P, 64], BF16, p0)
            prog.step("gpsimd", writes=[rw8], dma=True)(
                lambda e: e.dma_start(out=rw8.h[:, :],
                                      in_=w["L1_ffn_router_w"][:, :]))

            @prog.step("tensor", reads=[rw8, b.identb], writes=[prw])
            def _(e):
                for c in range(NCH):
                    ins = e.transpose(prw.h[:, c * 8:(c + 1) * 8],
                                      rw8.h[0:8, c * P:(c + 1) * P],
                                      b.identb.h[0:8, 0:8])
                return ins

            prog.step("vector", reads=[prw], writes=[rwT])(
                lambda e: e.tensor_copy(rwT.h[:, :], prw.h[:, :]))

            plog = [b.ps(tag + f"_plg{j}", [P, 8], F32, p0) for j in range(2)]
            ptab = [b.ps(tag + f"_ptb{j}", [8, P], F32, p0) for j in range(2)]
            mx = b.sb(tag + "_mx", [P, 1], F32, p0)
            nmx = b.sb(tag + "_nmx", [P, 1], F32, p0)
            ee = b.sb(tag + "_ee", [P, 8], F32, p0)
            ssum = b.sb(tag + "_ssm", [P, 1], F32, p0)
            rs = b.sb(tag + "_rs", [P, 1], F32, p0)
            probs = b.sb(tag + "_pb", [P, 8], F32, p0)
            m1 = b.sb(tag + "_m1", [P, 1], F32, p0)
            m2 = b.sb(tag + "_m2", [P, 1], F32, p0)
            eqt = b.sb(tag + "_eqt", [P, 8], F32, p0)
            msk1 = b.sb(tag + "_mk1", [P, 8], F32, p0)
            msk2 = b.sb(tag + "_mk2", [P, 8], F32, p0)
            den = b.sb(tag + "_den", [P, 1], F32, p0)
            rden = b.sb(tag + "_rdn", [P, 1], F32, p0)
            a1 = b.sb(tag + "_a1", [P, 1], F32, p0)
            tabf = b.sb(tag + "_tab", [P, 8], F32, p0)
            tb2 = b.sb(tag + "_tb2", [P, 8], F32, p0)
            for tt in range(NTT):
                pp = plog[tt % 2]

                @prog.step("tensor", reads=xnF + [rwT], writes=[pp])
                def _(e, tt=tt, pp=pp):
                    for c in range(NCH):
                        ins = e.matmul(pp.h[:, :],
                                       xnF[c].h[:, tt * P:(tt + 1) * P],
                                       rwT.h[:, c * 8:(c + 1) * 8],
                                       start=(c == 0), stop=(c == NCH - 1))
                    return ins

                prog.step("vector", reads=[pp], writes=[mx])(
                    lambda e, pp=pp: e.tensor_reduce(mx.h[:, :], pp.h[:, :],
                                                     axis=AX, op=ALU.max))
                prog.step("vector", reads=[mx], writes=[nmx])(
                    lambda e: e.tensor_scalar(nmx.h[:, :], mx.h[:, :], -1.0,
                                              None, ALU.mult))
                prog.step("scalar", reads=[pp, nmx], writes=[ee])(
                    lambda e, pp=pp: e.activation(ee.h[:, :], pp.h[:, :],
                                                  AF.Exp, bias=nmx.h[:, :],
                                                  scale=1.0))
                prog.step("vector", reads=[ee], writes=[ssum])(
                    lambda e: e.tensor_reduce(ssum.h[:, :], ee.h[:, :],
                                              axis=AX, op=ALU.add))

                @prog.step("vector", reads=[ssum], writes=[rs])
                def _(e):
                    b.vgap(e)
                    return e.reciprocal(rs.h[:, :], ssum.h[:, :])

                prog.step("vector", reads=[ee, rs], writes=[probs])(
                    lambda e: e.tensor_scalar(probs.h[:, :], ee.h[:, :],
                                              rs.h[:, 0:1], None, ALU.mult))
                prog.step("vector", reads=[probs], writes=[m1])(
                    lambda e: e.tensor_reduce(m1.h[:, :], probs.h[:, :],
                                              axis=AX, op=ALU.max))
                prog.step("vector", reads=[probs, m1], writes=[eqt])(
                    lambda e: e.tensor_scalar(eqt.h[:, :], probs.h[:, :],
                                              m1.h[:, 0:1], None, ALU.is_equal))
                prog.step("vector", reads=[eqt, im99], writes=[tb2])(
                    lambda e: e.scalar_tensor_tensor(
                        tb2.h[:, :], eqt.h[:, :], 99.0, im99.h[:, :],
                        ALU.bypass, ALU.mult))
                prog.step("vector", reads=[tb2], writes=[tb2])(
                    lambda e: e.tensor_scalar(tb2.h[:, :], tb2.h[:, :], 99.0,
                                              None, ALU.add))
                prog.step("vector", reads=[tb2], writes=[i1a])(
                    lambda e, tt=tt: e.tensor_reduce(
                        i1a.h[:, tt:tt + 1], tb2.h[:, :], axis=AX, op=ALU.min))
                prog.step("vector", reads=[iota8, i1a], writes=[msk1])(
                    lambda e, tt=tt: e.tensor_scalar(
                        msk1.h[:, :], iota8.h[:, :], i1a.h[:, tt:tt + 1],
                        None, ALU.is_equal))
                prog.step("vector", reads=[msk1, probs], writes=[tb2])(
                    lambda e: e.scalar_tensor_tensor(
                        tb2.h[:, :], msk1.h[:, :], -2.0, probs.h[:, :],
                        ALU.mult, ALU.add))
                prog.step("vector", reads=[tb2], writes=[m2])(
                    lambda e: e.tensor_reduce(m2.h[:, :], tb2.h[:, :],
                                              axis=AX, op=ALU.max))
                prog.step("vector", reads=[tb2, m2], writes=[eqt])(
                    lambda e: e.tensor_scalar(eqt.h[:, :], tb2.h[:, :],
                                              m2.h[:, 0:1], None, ALU.is_equal))
                prog.step("vector", reads=[eqt, im99], writes=[tb2])(
                    lambda e: e.scalar_tensor_tensor(
                        tb2.h[:, :], eqt.h[:, :], 99.0, im99.h[:, :],
                        ALU.bypass, ALU.mult))
                prog.step("vector", reads=[tb2], writes=[tb2])(
                    lambda e: e.tensor_scalar(tb2.h[:, :], tb2.h[:, :], 99.0,
                                              None, ALU.add))
                prog.step("vector", reads=[tb2], writes=[i2a])(
                    lambda e, tt=tt: e.tensor_reduce(
                        i2a.h[:, tt:tt + 1], tb2.h[:, :], axis=AX, op=ALU.min))
                prog.step("vector", reads=[iota8, i2a], writes=[msk2])(
                    lambda e, tt=tt: e.tensor_scalar(
                        msk2.h[:, :], iota8.h[:, :], i2a.h[:, tt:tt + 1],
                        None, ALU.is_equal))
                prog.step("vector", reads=[m1, m2], writes=[den])(
                    lambda e: e.tensor_tensor(den.h[:, :], m1.h[:, :],
                                              m2.h[:, :], ALU.add))
                prog.step("vector", reads=[den], writes=[den])(
                    lambda e: e.tensor_scalar(den.h[:, :], den.h[:, :], 1e-6,
                                              None, ALU.add))

                @prog.step("vector", reads=[den], writes=[rden])
                def _(e):
                    b.vgap(e)
                    return e.reciprocal(rden.h[:, :], den.h[:, :])

                prog.step("vector", reads=[m1, rden], writes=[w1a])(
                    lambda e, tt=tt: e.tensor_tensor(
                        w1a.h[:, tt:tt + 1], m1.h[:, :], rden.h[:, :], ALU.mult))
                prog.step("vector", reads=[m2, rden], writes=[w2a])(
                    lambda e, tt=tt: e.tensor_tensor(
                        w2a.h[:, tt:tt + 1], m2.h[:, :], rden.h[:, :], ALU.mult))
                prog.step("vector", reads=[w1a], writes=[a1])(
                    lambda e, tt=tt: e.tensor_scalar(
                        a1.h[:, :], w1a.h[:, tt:tt + 1], 1.0, None, ALU.add))
                prog.step("vector", reads=[msk1, a1], writes=[tabf])(
                    lambda e: e.tensor_scalar(tabf.h[:, :], msk1.h[:, :],
                                              a1.h[:, 0:1], None, ALU.mult))
                prog.step("vector", reads=[w2a], writes=[a1])(
                    lambda e, tt=tt: e.tensor_scalar(
                        a1.h[:, :], w2a.h[:, tt:tt + 1], 1.0, None, ALU.add))
                prog.step("vector", reads=[msk2, a1], writes=[tb2])(
                    lambda e: e.tensor_scalar(tb2.h[:, :], msk2.h[:, :],
                                              a1.h[:, 0:1], None, ALU.mult))
                prog.step("vector", reads=[tabf, tb2], writes=[tabf])(
                    lambda e: e.tensor_tensor(tabf.h[:, :], tabf.h[:, :],
                                              tb2.h[:, :], ALU.add))
                prog.step("vector", reads=[tabf], writes=[tabf])(
                    lambda e: e.tensor_scalar(tabf.h[:, :], tabf.h[:, :],
                                              -1.0, None, ALU.add))
                pt = ptab[tt % 2]
                prog.step("tensor", reads=[tabf, b.ident], writes=[pt])(
                    lambda e, pt=pt: e.transpose(pt.h[:, :], tabf.h[:, :],
                                                 b.ident.h[:, :]))
                prog.step("vector", reads=[pt], writes=[Wt])(
                    lambda e, pt=pt, tt=tt: e.tensor_copy(
                        Wt.h[:, tt * P:(tt + 1) * P], pt.h[0:8, 0:P]))
            prog.step("sync", reads=[Wt], writes=[wtpub], dma=True)(
                lambda e: e.dma_start(out=hb["wtpub"][:, :], in_=Wt.h[:, :]))
        prog.fence()
        if MOE_CUT < 2:
            return

        prog.step("gpsimd", reads=[wtpub], writes=[wtg], coll=True)(
            lambda e: e.collective_compute(
                "AllGather", ALU.bypass, replica_groups=[list(range(N_CORES))],
                ins=[hb["wtpub"].ap().opt()], outs=[hb["wtg"].ap().opt()]))
        prog.step("gpsimd", reads=[xnpub], writes=[xng], coll=True)(
            lambda e: e.collective_compute(
                "AllGather", ALU.bypass, replica_groups=[list(range(N_CORES))],
                ins=[hb["xnpub"].ap().opt()], outs=[hb["xng"].ap().opt()]))

        with contextlib.ExitStack() as pb:
            Wgs = b.sb(tag + "_wgs", [64, T], F32, pb)
            prog.step("sync", reads=[wtg], writes=[Wgs], dma=True)(
                lambda e: e.dma_start(out=Wgs.h[:, :], in_=hb["wtg"][:, :]))
            midb = b.sb(tag + "_mdb", [64, 1], F32, pb)
            mid = b.sb(tag + "_mid", [8, 1], F32, pb)
            mk = b.sb(tag + "_msk", [64, T], BF16, pb)
            rr = b.sb(tag + "_rr", [64, 1], F32, pb)
            d2 = b.sb(tag + "_d2", [64, 2], BF16, pb)
            cnt = b.sb(tag + "_cnt", [8, 1], F32, pb)
            ge = b.sb(tag + "_ge", [8, 1], F32, pb)
            gn = b.sb(tag + "_gn", [8, 1], F32, pb)
            dmi = b.sb(tag + "_dmi", [8, 1], F32, pb)
            dg1 = b.sb(tag + "_dg1", [64, 1], F32, pb)
            dg2 = b.sb(tag + "_dg2", [64, 1], F32, pb)
            pcs = b.sb(tag + "_pcs", [64, 2], F32, pb)
            pc = b.ps(tag + "_pc", [8, 2], F32, pb)
            prog.step("gpsimd", writes=[lo])(
                lambda e: e.memset(lo.h[:, :], -0.5))
            prog.step("gpsimd", writes=[hi])(
                lambda e: e.memset(hi.h[:, :], 1.0))

            def count_gt(thr_bc, dst, sel):
                prog.step("vector", reads=[Wgs, thr_bc], writes=[mk])(
                    lambda e: e.tensor_scalar(mk.h[:, :], Wgs.h[:, :],
                                              thr_bc.h[:, 0:1], None, ALU.is_gt))
                prog.step("vector", reads=[mk], writes=[rr])(
                    lambda e: e.tensor_reduce(rr.h[:, :], mk.h[:, :],
                                              axis=AX, op=ALU.add))
                emit_digits(b, dg1, dg2, rr, d2, 64.0, 64)
                prog.step("tensor", reads=[sel, d2], writes=[pc])(
                    lambda e: e.matmul(pc.h[:, :], sel.h[:, :], d2.h[:, :],
                                       start=True, stop=True))
                prog.step("vector", reads=[pc], writes=[pcs])(
                    lambda e: e.tensor_copy(pcs.h[0:8, :], pc.h[:, :]))
                prog.step("vector", reads=[pcs], writes=[dst])(
                    lambda e: e.scalar_tensor_tensor(
                        dst.h[:, :], pcs.h[0:8, 0:1], 64.0, pcs.h[0:8, 1:2],
                        ALU.mult, ALU.add))

            def bcast8(src, dst):
                for bb in range(8):
                    prog.step("sync", reads=[src], writes=[dst], dma=True)(
                        lambda e, bb=bb: e.dma_start(
                            out=dst.h[bb * 8:(bb + 1) * 8, 0:1],
                            in_=src.h[0:8, 0:1]))

            for it in range(BIS_IT):
                prog.step("vector", reads=[lo, hi], writes=[mid])(
                    lambda e: e.tensor_tensor(mid.h[:, :], lo.h[:, :],
                                              hi.h[:, :], ALU.add))
                prog.step("vector", reads=[mid], writes=[mid])(
                    lambda e: e.tensor_scalar(mid.h[:, :], mid.h[:, :], 0.5,
                                              None, ALU.mult))
                bcast8(mid, midb)
                count_gt(midb, cnt, sel64)
                prog.step("vector", reads=[cnt], writes=[ge])(
                    lambda e: e.tensor_scalar(ge.h[:, :], cnt.h[:, :],
                                              float(C_CAP) - 0.5, None, ALU.is_gt))
                prog.step("vector", reads=[mid, lo], writes=[dmi])(
                    lambda e: e.tensor_tensor(dmi.h[:, :], mid.h[:, :],
                                              lo.h[:, :], ALU.subtract))
                prog.step("vector", reads=[ge, dmi], writes=[dmi])(
                    lambda e: e.tensor_tensor(dmi.h[:, :], ge.h[:, :],
                                              dmi.h[:, :], ALU.mult))
                prog.step("vector", reads=[lo, dmi], writes=[lo])(
                    lambda e: e.tensor_tensor(lo.h[:, :], lo.h[:, :],
                                              dmi.h[:, :], ALU.add))
                prog.step("vector", reads=[ge], writes=[gn])(
                    lambda e: e.tensor_scalar(gn.h[:, :], ge.h[:, :], -1.0,
                                              1.0, ALU.mult, ALU.add))
                prog.step("vector", reads=[mid, hi], writes=[dmi])(
                    lambda e: e.tensor_tensor(dmi.h[:, :], mid.h[:, :],
                                              hi.h[:, :], ALU.subtract))
                prog.step("vector", reads=[gn, dmi], writes=[dmi])(
                    lambda e: e.tensor_tensor(dmi.h[:, :], gn.h[:, :],
                                              dmi.h[:, :], ALU.mult))
                prog.step("vector", reads=[hi, dmi], writes=[hi])(
                    lambda e: e.tensor_tensor(hi.h[:, :], hi.h[:, :],
                                              dmi.h[:, :], ALU.add))

            hib = b.sb(tag + "_hib", [64, 1], F32, pb)
            lob = b.sb(tag + "_lob", [64, 1], F32, pb)
            bcast8(hi, hib)
            bcast8(lo, lob)
            strc = b.sb(tag + "_stc", [8, 1], F32, pb)
            count_gt(hib, strc, sel64)
            prog.step("vector", reads=[strc], writes=[Rbud])(
                lambda e: e.tensor_scalar(Rbud.h[:, :], strc.h[:, :], -1.0,
                                          float(C_CAP), ALU.mult, ALU.add))
            smk = b.sb(tag + "_smk", [64, T], BF16, pb)
            str_rr = b.sb(tag + "_srr", [64, 1], F32, pb)
            tie_rr = b.sb(tag + "_trr", [64, 1], F32, pb)
            trow = b.sb(tag + "_trw", [64, 1], F32, pb)
            tkrow = b.sb(tag + "_tkr2", [64, 1], F32, pb)
            krow = b.sb(tag + "_krw", [64, 1], F32, pb)
            Rb64 = b.sb(tag + "_rb64", [64, 1], F32, pb)
            pc2 = b.ps(tag + "_pc2", [64, 2], F32, pb)
            prog.step("vector", reads=[Wgs, hib], writes=[smk])(
                lambda e: e.tensor_scalar(smk.h[:, :], Wgs.h[:, :],
                                          hib.h[:, 0:1], None, ALU.is_gt))
            prog.step("vector", reads=[smk], writes=[str_rr])(
                lambda e: e.tensor_reduce(str_rr.h[:, :], smk.h[:, :],
                                          axis=AX, op=ALU.add))
            prog.step("vector", reads=[Wgs, lob], writes=[mk])(
                lambda e: e.tensor_scalar(mk.h[:, :], Wgs.h[:, :],
                                          lob.h[:, 0:1], None, ALU.is_gt))
            prog.step("vector", reads=[mk], writes=[tie_rr])(
                lambda e: e.tensor_reduce(tie_rr.h[:, :], mk.h[:, :],
                                          axis=AX, op=ALU.add))
            prog.step("vector", reads=[tie_rr, str_rr], writes=[tie_rr])(
                lambda e: e.tensor_tensor(tie_rr.h[:, :], tie_rr.h[:, :],
                                          str_rr.h[:, :], ALU.subtract))
            emit_digits(b, dg1, dg2, tie_rr, d2, 64.0, 64)
            prog.step("tensor", reads=[cummask, d2], writes=[pc])(
                lambda e: e.matmul(pc.h[:, :], cummask.h[:, :], d2.h[:, :],
                                   start=True, stop=True))
            prog.step("vector", reads=[pc], writes=[pcs])(
                lambda e: e.tensor_copy(pcs.h[0:8, :], pc.h[:, :]))
            prog.step("vector", reads=[pcs], writes=[toffs])(
                lambda e: e.scalar_tensor_tensor(
                    toffs.h[:, :], pcs.h[0:8, 0:1], 64.0, pcs.h[0:8, 1:2],
                    ALU.mult, ALU.add))
            prog.step("tensor", reads=[selcum, d2], writes=[pc2])(
                lambda e: e.matmul(pc2.h[:, :], selcum.h[:, :], d2.h[:, :],
                                   start=True, stop=True))
            prog.step("vector", reads=[pc2], writes=[pcs])(
                lambda e: e.tensor_copy(pcs.h[:, :], pc2.h[:, :]))
            prog.step("vector", reads=[pcs], writes=[trow])(
                lambda e: e.scalar_tensor_tensor(
                    trow.h[:, :], pcs.h[:, 0:1], 64.0, pcs.h[:, 1:2],
                    ALU.mult, ALU.add))
            bcast8(Rbud, Rb64)
            prog.step("vector", reads=[Rb64, trow], writes=[tkrow])(
                lambda e: e.tensor_tensor(tkrow.h[:, :], Rb64.h[:, :],
                                          trow.h[:, :], ALU.subtract))
            prog.step("vector", reads=[tkrow], writes=[tkrow])(
                lambda e: e.tensor_scalar(tkrow.h[:, :], tkrow.h[:, :], 0.0,
                                          None, ALU.max))
            prog.step("vector", reads=[tkrow, tie_rr], writes=[tkrow])(
                lambda e: e.tensor_tensor(tkrow.h[:, :], tkrow.h[:, :],
                                          tie_rr.h[:, :], ALU.min))
            prog.step("vector", reads=[str_rr, tkrow], writes=[krow])(
                lambda e: e.tensor_tensor(krow.h[:, :], str_rr.h[:, :],
                                          tkrow.h[:, :], ALU.add))
            emit_digits(b, dg1, dg2, krow, d2, 64.0, 64)
            prog.step("tensor", reads=[cummask, d2], writes=[pc])(
                lambda e: e.matmul(pc.h[:, :], cummask.h[:, :], d2.h[:, :],
                                   start=True, stop=True))
            prog.step("vector", reads=[pc], writes=[pcs])(
                lambda e: e.tensor_copy(pcs.h[0:8, :], pc.h[:, :]))
            prog.step("vector", reads=[pcs], writes=[offs8])(
                lambda e: e.scalar_tensor_tensor(
                    offs8.h[:, :], pcs.h[0:8, 0:1], 64.0, pcs.h[0:8, 1:2],
                    ALU.mult, ALU.add))

            mz8 = b.sb(tag + "_mz8", [8, T], F32, pb)
            s8 = b.sb(tag + "_s8", [8, T], F32, pb)
            t8 = b.sb(tag + "_t8", [8, T], F32, pb)
            sc8 = b.sb(tag + "_sc8", [8, T], F32, pb)
            packed = b.sb(tag + "_pk8", [8, T], F32, pb)
            prog.step("gpsimd", writes=[mz8])(
                lambda e: e.memset(mz8.h[:, :], 0.0))
            prog.step("vector", reads=[Wt, hi], writes=[s8])(
                lambda e: e.tensor_scalar(s8.h[:, :], Wt.h[:, :],
                                          hi.h[:, 0:1], None, ALU.is_gt))
            prog.step("vector", reads=[Wt, lo], writes=[t8])(
                lambda e: e.tensor_scalar(t8.h[:, :], Wt.h[:, :],
                                          lo.h[:, 0:1], None, ALU.is_gt))
            prog.step("vector", reads=[t8, s8], writes=[t8])(
                lambda e: e.tensor_tensor(t8.h[:, :], t8.h[:, :], s8.h[:, :],
                                          ALU.subtract))
            prog.step("vector", reads=[t8, mz8], writes=[sc8])(
                lambda e: e.tensor_tensor_scan(sc8.h[:, :], t8.h[:, :],
                                               mz8.h[:, :], 0.0,
                                               ALU.add, ALU.add))
            prog.step("vector", reads=[sc8, t8], writes=[sc8])(
                lambda e: e.tensor_tensor(sc8.h[:, :], sc8.h[:, :], t8.h[:, :],
                                          ALU.subtract))
            prog.step("vector", reads=[sc8, toffs], writes=[sc8])(
                lambda e: e.tensor_scalar(sc8.h[:, :], sc8.h[:, :],
                                          toffs.h[:, 0:1], None, ALU.add))
            prog.step("vector", reads=[sc8, Rbud], writes=[sc8])(
                lambda e: e.tensor_scalar(sc8.h[:, :], sc8.h[:, :],
                                          Rbud.h[:, 0:1], None, ALU.is_lt))
            prog.step("vector", reads=[t8, sc8], writes=[t8])(
                lambda e: e.tensor_tensor(t8.h[:, :], t8.h[:, :], sc8.h[:, :],
                                          ALU.mult))
            prog.step("vector", reads=[s8, t8], writes=[s8])(
                lambda e: e.tensor_tensor(s8.h[:, :], s8.h[:, :], t8.h[:, :],
                                          ALU.add))
            prog.step("vector", reads=[s8, mz8], writes=[sc8])(
                lambda e: e.tensor_tensor_scan(sc8.h[:, :], s8.h[:, :],
                                               mz8.h[:, :], 0.0,
                                               ALU.add, ALU.add))
            prog.step("vector", reads=[sc8, s8], writes=[sc8])(
                lambda e: e.tensor_tensor(sc8.h[:, :], sc8.h[:, :], s8.h[:, :],
                                          ALU.subtract))
            prog.step("vector", reads=[sc8, offs8], writes=[sc8])(
                lambda e: e.tensor_scalar(sc8.h[:, :], sc8.h[:, :],
                                          offs8.h[:, 0:1], None, ALU.add))
            prog.step("vector", reads=[s8, sc8], writes=[packed])(
                lambda e: e.scalar_tensor_tensor(
                    packed.h[:, :], s8.h[:, :], 8192.0, sc8.h[:, :],
                    ALU.mult, ALU.add))
            if b.mtap:
                b.mtap("Wt", Wt, 8)
                b.mtap("packed", packed, 8)
                b.mtap("hi", hi, 8, width=1)
                b.mtap("lo", lo, 8, width=1)
                b.mtap("strc", strc, 8, width=1)
                b.mtap("offs8", offs8, 8, width=1)
                b.mtap("toffs", toffs, 8, width=1)
                b.mtap("kept8", s8, 8)
                b.mtap("pos8", sc8, 8)

            ptk = [b.ps(tag + f"_ptk{j}", [P, 8], F32, pb) for j in range(2)]
            kk = b.sb(tag + "_kk", [P, 8], F32, pb)
            posT = b.sb(tag + "_psT", [P, 8], F32, pb)
            p1 = b.sb(tag + "_p1", [P, 1], F32, pb)
            k1 = b.sb(tag + "_k1", [P, 1], F32, pb)
            s1t = b.sb(tag + "_s1t", [P, 1], F32, pb)
            qq1 = b.sb(tag + "_qq1", [P, 1], F32, pb)
            rm1 = b.sb(tag + "_rm1", [P, 1], F32, pb)
            sent = b.sb(tag + "_snt", [P, 1], F32, pb)
            gbase = b.sb(tag + "_gbs", [P, 1], F32, pb)
            gval = b.sb(tag + "_gvl", [P, 1], F32, pb)
            ridx = [b.sb(tag + f"_ri{j}", [P, 1], I32, pb) for j in range(2)]
            tsel = b.sb(tag + "_tsl", [P, 8], F32, pb)
            zrow = b.sb(tag + "_zrw", [P, 161], F32, pb)
            prog.step("gpsimd", writes=[zrow])(
                lambda e: e.memset(zrow.h[:, :], 0.0))
            prog.step("sync", reads=[zrow], writes=[stin], dma=True)(
                lambda e: e.dma_start(out=hb["stin"][:, :], in_=zrow.h[:, :]))
            prog.step("vector", reads=[piota], writes=[sent])(
                lambda e: e.tensor_scalar(sent.h[:, :], piota.h[:, :], 161.0,
                                          160.0, ALU.mult, ALU.add))
            prog.step("vector", reads=[coreid, piota], writes=[gbase])(
                lambda e: e.scalar_tensor_tensor(
                    gbase.h[:, :], coreid.h[:, :], float(T), piota.h[:, :],
                    ALU.mult, ALU.add))
            for tt in range(NTT):
                pt = ptk[tt % 2]
                prog.step("tensor", reads=[packed, b.ident], writes=[pt])(
                    lambda e, pt=pt, tt=tt: e.transpose(
                        pt.h[:, :], packed.h[:, tt * P:(tt + 1) * P],
                        b.ident.h[0:8, 0:8]))
                prog.step("vector", reads=[pt], writes=[kk])(
                    lambda e, pt=pt: e.tensor_scalar(
                        kk.h[:, :], pt.h[0:P, 0:8], 8191.5, None, ALU.is_gt))
                prog.step("vector", reads=[kk, pt], writes=[posT])(
                    lambda e, pt=pt: e.scalar_tensor_tensor(
                        posT.h[:, :], kk.h[:, :], -8192.0, pt.h[0:P, 0:8],
                        ALU.mult, ALU.add))
                for (ia, wa, wk, sxa, j) in ((i1a, w1a, w1k, sx1, 0),
                                             (i2a, w2a, w2k, sx2, 1)):
                    prog.step("vector", reads=[iota8, ia], writes=[tsel])(
                        lambda e, ia=ia, tt=tt: e.tensor_scalar(
                            tsel.h[:, :], iota8.h[:, :], ia.h[:, tt:tt + 1],
                            None, ALU.is_equal))
                    prog.step("vector", reads=[posT, tsel], writes=[tsel])(
                        lambda e: e.tensor_tensor(tsel.h[:, :], posT.h[:, :],
                                                  tsel.h[:, :], ALU.mult))
                    prog.step("vector", reads=[tsel], writes=[p1])(
                        lambda e: e.tensor_reduce(p1.h[:, :], tsel.h[:, :],
                                                  axis=AX, op=ALU.add))
                    prog.step("vector", reads=[iota8, ia], writes=[tsel])(
                        lambda e, ia=ia, tt=tt: e.tensor_scalar(
                            tsel.h[:, :], iota8.h[:, :], ia.h[:, tt:tt + 1],
                            None, ALU.is_equal))
                    prog.step("vector", reads=[kk, tsel], writes=[tsel])(
                        lambda e: e.tensor_tensor(tsel.h[:, :], kk.h[:, :],
                                                  tsel.h[:, :], ALU.mult))
                    prog.step("vector", reads=[tsel], writes=[k1])(
                        lambda e: e.tensor_reduce(k1.h[:, :], tsel.h[:, :],
                                                  axis=AX, op=ALU.add))
                    prog.step("vector", reads=[ia, p1], writes=[s1t])(
                        lambda e, ia=ia, tt=tt: e.scalar_tensor_tensor(
                            s1t.h[:, :], ia.h[:, tt:tt + 1], float(C_CAP),
                            p1.h[:, :], ALU.mult, ALU.add))
                    prog.step("vector", reads=[s1t, k1], writes=[sxa])(
                        lambda e, sxa=sxa, tt=tt: e.tensor_tensor(
                            sxa.h[:, tt:tt + 1], s1t.h[:, :], k1.h[:, :],
                            ALU.mult))
                    prog.step("vector", reads=[k1, wa], writes=[wk])(
                        lambda e, wa=wa, wk=wk, tt=tt: e.scalar_tensor_tensor(
                            wk.h[:, tt:tt + 1], k1.h[:, :], 0.5,
                            wa.h[:, tt:tt + 1], ALU.mult, ALU.mult))
                    prog.step("vector", reads=[s1t], writes=[qq1])(
                        lambda e: e.tensor_scalar(
                            qq1.h[:, :], s1t.h[:, :], 1.0 / 128.0,
                            -0.49609375, ALU.mult, ALU.add))
                    prog.step("vector", reads=[qq1], writes=[qq1])(
                        lambda e: e.tensor_scalar(qq1.h[:, :], qq1.h[:, :],
                                                  MAGIC, -MAGIC,
                                                  ALU.add, ALU.add))
                    prog.step("vector", reads=[qq1, s1t], writes=[rm1])(
                        lambda e: e.scalar_tensor_tensor(
                            rm1.h[:, :], qq1.h[:, :], -128.0, s1t.h[:, :],
                            ALU.mult, ALU.add))
                    prog.step("vector", reads=[rm1, qq1], writes=[rm1])(
                        lambda e: e.scalar_tensor_tensor(
                            rm1.h[:, :], rm1.h[:, :], 161.0, qq1.h[:, :],
                            ALU.mult, ALU.add))
                    prog.step("vector", reads=[rm1, sent], writes=[rm1])(
                        lambda e: e.tensor_tensor(rm1.h[:, :], rm1.h[:, :],
                                                  sent.h[:, :], ALU.subtract))
                    prog.step("vector", reads=[k1, rm1], writes=[rm1])(
                        lambda e: e.tensor_tensor(rm1.h[:, :], k1.h[:, :],
                                                  rm1.h[:, :], ALU.mult))
                    prog.step("vector", reads=[rm1, sent], writes=[rm1])(
                        lambda e: e.tensor_tensor(rm1.h[:, :], rm1.h[:, :],
                                                  sent.h[:, :], ALU.add))
                    ri = ridx[j]
                    prog.step("vector", reads=[rm1], writes=[ri])(
                        lambda e, ri=ri: e.tensor_copy(ri.h[:, :], rm1.h[:, :]))
                    prog.step("vector", reads=[gbase], writes=[gval])(
                        lambda e, tt=tt: e.tensor_scalar(
                            gval.h[:, :], gbase.h[:, :], float(tt * P + 1),
                            None, ALU.add))
                    prog.step("gpsimd", reads=[ri, gval], writes=[stin],
                              dma=True)(
                        lambda e, ri=ri: e.indirect_dma_start(
                            hb["stin"][:, :],
                            bass.IndirectOffsetOnAxis(ri.h[:, 0:1], 0),
                            gval.h[:, :], None))
            if b.mtap:
                b.mtap("i1a", i1a, P, width=16)
                b.mtap("i2a", i2a, P, width=16)
                b.mtap("sx1", sx1, P, width=16)
                b.mtap("sx2", sx2, P, width=16)
        prog.fence()
        if MOE_CUT < 3:
            return

        prog.step("gpsimd", reads=[stin], writes=[stout], coll=True)(
            lambda e: e.collective_compute(
                "AllReduce", ALU.add, replica_groups=[list(range(N_CORES))],
                ins=[hb["stin"].ap().opt()], outs=[hb["stout"].ap().opt()]))

        pe = contextlib.ExitStack()
        dispT = [b.sb(tag + f"_dT{fc}", [P, C_CAP], BF16, pe)
                 for fc in range(NCH)]
        with contextlib.ExitStack() as pd:
            rt = b.sb(tag + "_rt", [P, 161], F32, pd)
            prog.step("sync", reads=[stout], writes=[rt], dma=True)(
                lambda e: e.dma_start(out=rt.h[:, :], in_=hb["stout"][:, :]))
            it161 = b.sb(tag + "_i161", [P, 161], F32, pd)
            cmv = b.sb(tag + "_cmv", [P, 161], F32, pd)
            colsel = b.sb(tag + "_csl", [P, 1], F32, pd)
            tokv = b.sb(tag + "_tkv", [P, 1], F32, pd)
            gidx = [b.sb(tag + f"_gi{j}", [P, 1], I32, pd) for j in range(2)]
            disp = [b.sb(tag + f"_dp{j}", [P, D], BF16, pd) for j in range(2)]
            ptd = [b.ps(tag + f"_ptd{j}", [P, D], BF16, pd) for j in range(2)]
            prog.step("gpsimd", writes=[it161])(
                lambda e: e.iota(it161.h[:, :], [[1, 161]], channel_multiplier=0,
                                 allow_small_or_imprecise_dtypes=True))
            for ti in range(NTILE):
                prog.step("vector", reads=[coreid], writes=[colsel])(
                    lambda e, ti=ti: e.tensor_scalar(
                        colsel.h[:, :], coreid.h[:, :], float(NTILE),
                        float(ti), ALU.mult, ALU.add))
                prog.step("vector", reads=[it161, colsel], writes=[cmv])(
                    lambda e: e.tensor_scalar(cmv.h[:, :], it161.h[:, :],
                                              colsel.h[:, 0:1], None,
                                              ALU.is_equal))
                prog.step("vector", reads=[rt, cmv], writes=[cmv])(
                    lambda e: e.tensor_tensor(cmv.h[:, :], rt.h[:, :],
                                              cmv.h[:, :], ALU.mult))
                prog.step("vector", reads=[cmv], writes=[tokv])(
                    lambda e: e.tensor_reduce(tokv.h[:, :], cmv.h[:, :],
                                              axis=AX, op=ALU.add))
                gi = gidx[ti % 2]
                prog.step("vector", reads=[tokv], writes=[tokv])(
                    lambda e: e.tensor_scalar(tokv.h[:, :], tokv.h[:, :],
                                              -1.0, 0.0, ALU.add, ALU.max))
                prog.step("vector", reads=[tokv], writes=[gi])(
                    lambda e, gi=gi: e.tensor_copy(gi.h[:, :], tokv.h[:, :]))
                dp = disp[ti % 2]
                prog.step("gpsimd", reads=[gi, xng], writes=[dp], dma=True)(
                    lambda e, gi=gi, dp=dp: e.indirect_dma_start(
                        dp.h[:, :], None, hb["xng"][:, :],
                        bass.IndirectOffsetOnAxis(gi.h[:, 0:1], 0)))
                pp = ptd[ti % 2]

                @prog.step("tensor", reads=[dp, b.identb], writes=[pp])
                def _(e, dp=dp, pp=pp):
                    for fc in range(NCH):
                        ins = e.transpose(pp.h[:, fc * P:(fc + 1) * P],
                                          dp.h[:, fc * P:(fc + 1) * P],
                                          b.identb.h[:, :])
                    return ins

                @prog.step("vector", reads=[pp], writes=dispT)
                def _(e, pp=pp, ti=ti):
                    for fc in range(NCH):
                        ins = e.tensor_copy(
                            dispT[fc].h[:, ti * P:(ti + 1) * P],
                            pp.h[:, fc * P:(fc + 1) * P])
                    return ins
        prog.fence()
        if MOE_CUT < 4:
            pe.close()
            return

        NCC = 4096 // TS
        ACC = [b.sb(tag + f"_ac{i}", [P, D], BF16, pe) for i in range(NTILE)]
        wgc = [b.sb(tag + f"_wg{j}", [P, 4096], BF16, pe) for j in range(2)]
        wuc = [b.sb(tag + f"_wu{j}", [P, 4096], BF16, pe) for j in range(2)]
        wdc = b.sb(tag + "_wd", [P, 4096], BF16, pe)
        sgf = b.sb(tag + "_sgf", [P, TS], F32, pe)
        hsb = b.sb(tag + "_hsb", [P, TS], BF16, pe)
        hTs = b.sb(tag + "_hTs", [P, TS], BF16, pe)
        pg = b.ps(tag + "_pg", [P, TS], F32, pe)
        pu = b.ps(tag + "_pu", [P, TS], F32, pe)
        pht = b.ps(tag + "_pht", [P, TS], BF16, pe)
        pout = b.ps(tag + "_po", [P, D], F32, pe)
        for cc in range(NCC):
            wg_t = wgc[cc % 2]
            wu_t = wuc[cc % 2]
            for fc in range(NCH):
                prog.step("gpsimd", writes=[wg_t], dma=True)(
                    lambda e, fc=fc, cc=cc, wg_t=wg_t: e.dma_start(
                        out=wg_t.h[:, fc * TS:(fc + 1) * TS],
                        in_=w["L1_ffn_wg"][fc * P:(fc + 1) * P,
                                           cc * TS:(cc + 1) * TS]))
                prog.step("gpsimd", writes=[wu_t], dma=True)(
                    lambda e, fc=fc, cc=cc, wu_t=wu_t: e.dma_start(
                        out=wu_t.h[:, fc * TS:(fc + 1) * TS],
                        in_=w["L1_ffn_wu"][fc * P:(fc + 1) * P,
                                           cc * TS:(cc + 1) * TS]))
            for s in range(4):
                prog.step("gpsimd", writes=[wdc], dma=True)(
                    lambda e, s=s, cc=cc: e.dma_start(
                        out=wdc.h[:, s * D:(s + 1) * D],
                        in_=w["L1_ffn_wd"][cc * TS + s * P:cc * TS + (s + 1) * P, :]))
            for ti in range(NTILE):
                @prog.step("tensor", reads=dispT + [wg_t], writes=[pg])
                def _(e, ti=ti, wg_t=wg_t):
                    for fc in range(NCH):
                        ins = e.matmul(pg.h[:, :],
                                       dispT[fc].h[:, ti * P:(ti + 1) * P],
                                       wg_t.h[:, fc * TS:(fc + 1) * TS],
                                       start=(fc == 0), stop=(fc == NCH - 1))
                    return ins

                @prog.step("tensor", reads=dispT + [wu_t], writes=[pu])
                def _(e, ti=ti, wu_t=wu_t):
                    for fc in range(NCH):
                        ins = e.matmul(pu.h[:, :],
                                       dispT[fc].h[:, ti * P:(ti + 1) * P],
                                       wu_t.h[:, fc * TS:(fc + 1) * TS],
                                       start=(fc == 0), stop=(fc == NCH - 1))
                    return ins

                prog.step("scalar", reads=[pg], writes=[sgf])(
                    lambda e: e.activation(sgf.h[:, :], pg.h[:, :], AF.Silu))
                prog.step("vector", reads=[sgf, pu], writes=[hsb])(
                    lambda e: e.tensor_tensor(hsb.h[:, :], sgf.h[:, :],
                                              pu.h[:, :], ALU.mult))

                @prog.step("tensor", reads=[hsb, b.identb], writes=[pht])
                def _(e):
                    for s in range(4):
                        ins = e.transpose(pht.h[:, s * P:(s + 1) * P],
                                          hsb.h[:, s * P:(s + 1) * P],
                                          b.identb.h[:, :])
                    return ins

                prog.step("vector", reads=[pht], writes=[hTs])(
                    lambda e: e.tensor_copy(hTs.h[:, :], pht.h[:, :]))

                @prog.step("tensor", reads=[hTs, wdc], writes=[pout])
                def _(e):
                    for hh in range(2):
                        for s in range(4):
                            ins = e.matmul(pout.h[:, hh * TS:(hh + 1) * TS],
                                           hTs.h[:, s * P:(s + 1) * P],
                                           wdc.h[:, s * D + hh * TS:
                                                 s * D + (hh + 1) * TS],
                                           start=(s == 0), stop=(s == 3))
                    return ins

                ac = ACC[ti]
                if cc == 0:
                    prog.step("vector", reads=[pout], writes=[ac])(
                        lambda e, ac=ac: e.tensor_copy(ac.h[:, :], pout.h[:, :]))
                else:
                    prog.step("vector", reads=[pout, ac], writes=[ac])(
                        lambda e, ac=ac: e.tensor_tensor(
                            ac.h[:, :], ac.h[:, :], pout.h[:, :], ALU.add))
        for ti in range(NTILE):
            prog.step("sync", reads=[ACC[ti]], writes=[pubout], dma=True)(
                lambda e, ti=ti: e.dma_start(
                    out=hb["pubout"][ti * P:(ti + 1) * P, :],
                    in_=ACC[ti].h[:, :]))
        pe.close()
        prog.fence()
        if MOE_CUT < 5:
            return

        prog.step("gpsimd", reads=[pubout], writes=[outg], coll=True)(
            lambda e: e.collective_compute(
                "AllGather", ALU.bypass, replica_groups=[list(range(N_CORES))],
                ins=[hb["pubout"].ap().opt()], outs=[hb["outg"].ap().opt()]))
        with contextlib.ExitStack() as pr:
            go = [b.sb(tag + f"_go{j}", [P, D], BF16, pr) for j in range(2)]
            gid = [b.sb(tag + f"_gd{j}", [P, 1], I32, pr) for j in range(2)]
            c1 = b.sb(tag + "_c1", [P, D], F32, pr)
            c2 = b.sb(tag + "_c2", [P, D], F32, pr)
            pxt = [b.ps(tag + f"_pxt{j}", [P, D], F32, pr) for j in range(2)]
            for tt in range(NTT):
                for j, sxa in ((0, sx1), (1, sx2)):
                    gi = gid[j]
                    prog.step("vector", reads=[sxa], writes=[gi])(
                        lambda e, gi=gi, sxa=sxa, tt=tt: e.tensor_copy(
                            gi.h[:, :], sxa.h[:, tt:tt + 1]))
                    prog.step("gpsimd", reads=[gi, outg], writes=[go[j]],
                              dma=True)(
                        lambda e, gi=gi, j=j: e.indirect_dma_start(
                            go[j].h[:, :], None, hb["outg"][:, :],
                            bass.IndirectOffsetOnAxis(gi.h[:, 0:1], 0)))
                prog.step("vector", reads=[go[0], w1k], writes=[c1])(
                    lambda e, tt=tt: e.tensor_scalar(
                        c1.h[:, :], go[0].h[:, :], w1k.h[:, tt:tt + 1],
                        None, ALU.mult))
                prog.step("vector", reads=[go[1], w2k], writes=[c2])(
                    lambda e, tt=tt: e.tensor_scalar(
                        c2.h[:, :], go[1].h[:, :], w2k.h[:, tt:tt + 1],
                        None, ALU.mult))
                prog.step("vector", reads=[c1, c2], writes=[c1])(
                    lambda e: e.tensor_tensor(c1.h[:, :], c1.h[:, :],
                                              c2.h[:, :], ALU.add))
                pp = pxt[tt % 2]

                @prog.step("tensor", reads=[c1, b.ident], writes=[pp])
                def _(e, pp=pp):
                    for fc in range(NCH):
                        ins = e.transpose(pp.h[:, fc * P:(fc + 1) * P],
                                          c1.h[:, fc * P:(fc + 1) * P],
                                          b.ident.h[:, :])
                    return ins

                @prog.step("vector", reads=[pp] + X, writes=X)
                def _(e, pp=pp, tt=tt):
                    for fc in range(NCH):
                        ins = e.tensor_tensor(
                            X[fc].h[:, tt * P:(tt + 1) * P],
                            X[fc].h[:, tt * P:(tt + 1) * P],
                            pp.h[:, fc * P:(fc + 1) * P], ALU.add)
                    return ins



WSHAPES = {
    "L0_attn_conv_weight": [D, 4], "L0_attn_conv_bias": [1, D],
    "L0_attn_Wr_w": [D, D], "L0_attn_Wr_b": [1, D], "L0_attn_Wi_w": [D, D],
    "L0_attn_Wi_b": [1, D], "L0_attn_log_a": [1, D], "L0_attn_out_w": [D, D],
    "L0_attn_norm_w": [1, D], "L0_ffn_weight": [D, D],
    "L1_attn_Wq": [D, D], "L1_attn_Wk": [D, D], "L1_attn_Wv": [D, D],
    "L1_attn_Wo": [D, D], "L1_attn_W_down_kv": [D, 256],
    "L1_attn_W_up_k": [256, D], "L1_attn_W_up_v": [256, D],
    "L1_ffn_router_w": [8, D], "L1_ffn_wg": [D, 4096],
    "L1_ffn_wu": [D, 4096], "L1_ffn_wd": [4096, D],
}
WNAMES = list(WSHAPES)

CONSTS = {
    "ident_f32": ([P, P], F32),
    "ident_bf16": ([P, P], BF16),
    "edecm": ([P, T], BF16),
    "edecp": ([P, T], BF16),
    "corr": ([P, P], F32),
    "sel64": ([64, 8], BF16),
    "selcum": ([64, 64], BF16),
    "coreid": ([P, 1], F32),
    "cummask": ([64, 8], BF16),
}

N_SUBLAYERS = 6
GRIFFIN_CUT = 99
ATTN_CUT = 99
ATTN_SUB = 99
ATTN_FIRST = False
DEV_TAPS = False
MOE_TAPS = False
LAST_MTAP_ROWS = {}


def build_nc(n_sub=None):
    if n_sub is None:
        n_sub = N_SUBLAYERS
    nc = bass.Bass("TRN2", target_bir_lowering=False,
                   detect_race_conditions=False)
    x_hbm = nc.dram_tensor("x", [T, D], F32, kind="ExternalInput")
    y_hbm = nc.dram_tensor("y", [T, D], F32, kind="ExternalOutput")
    dbg_hbm = nc.dram_tensor("dbg", [16, T], F32, kind="ExternalOutput") if DEV_TAPS else None
    w_hbm = {n: nc.dram_tensor(n, WSHAPES[n], F32, kind="ExternalInput")
             for n in WNAMES}
    xln_hbm = nc.dram_tensor("xln_scratch", [D, T], BF16)
    c_hbm = {n: nc.dram_tensor(n, list(s), d, kind="ExternalInput")
             for n, (s, d) in CONSTS.items()}
    moe_hbm = {}
    for tg in ("m0", "m1"):
        moe_hbm[tg] = {
            "xnpub": nc.dram_tensor(tg + "_xnpub", [T, D], BF16),
            "xng": nc.dram_tensor(tg + "_xng", [N_CORES * T, D], BF16),
            "wtpub": nc.dram_tensor(tg + "_wtpub", [8, T], F32),
            "wtg": nc.dram_tensor(tg + "_wtg", [64, T], F32),
            "stin": nc.dram_tensor(tg + "_stin", [NS_TOT, 1], F32),
            "stout": nc.dram_tensor(tg + "_stout", [NS_TOT, 1], F32),
            "pubout": nc.dram_tensor(tg + "_pubout", [C_CAP, D], BF16),
            "outg": nc.dram_tensor(tg + "_outg", [NSLOT, D], BF16),
        }

    with contextlib.ExitStack() as es:
        prog = Prog(nc, es)
        b = Build(nc, es, prog)
        b.w_hbm = w_hbm
        b.c_hbm = c_hbm
        b.xln_hbm = xln_hbm
        b.moe_hbm = moe_hbm
        b.mtap = None
        b.mtap_rows = {}
        if MOE_TAPS:
            mdbg = nc.dram_tensor("moedbg", [640, T], F32, kind="ExternalOutput")
            midx = [0]

            def mtap(name, buf, nrows, width=T):
                i = midx[0]
                midx[0] += nrows
                b.mtap_rows[name] = (i, nrows, width)
                prog.step("sync", reads=[buf], dma=True)(
                    lambda e, i=i: e.dma_start(
                        out=mdbg[i:i + nrows, 0:width],
                        in_=buf.h[0:nrows, 0:width],
                        allow_slow_non_contiguous=True))
            b.mtap = mtap
        b.dbg_hbm = dbg_hbm
        b.dbg_idx = [0]

        def tap(buf, n=1):
            if dbg_hbm is None:
                return
            i = b.dbg_idx[0]
            b.dbg_idx[0] += n
            prog.step("sync", reads=[buf], dma=True)(
                lambda e: e.dma_start(out=dbg_hbm[i:i + n, :].rearrange("a b -> 1 (a b)") if False else dbg_hbm[i:i + n, 0:T],
                                      in_=buf.h[0:n, 0:T]))
        b.tap = tap
        emit_entry(b, x_hbm, c_hbm)
        prog.fence()
        if n_sub >= 1:
            emit_l0ffn(b, "f0a")
            prog.fence()
        if n_sub >= 2:
            emit_griffin(b)
            prog.fence()
        if n_sub >= 3:
            emit_l0ffn(b, "f0b")
            prog.fence()
        if n_sub >= 4:
            if ATTN_FIRST:
                emit_l1attn(b)
            else:
                emit_l1ffn(b, "m0")
            prog.fence()
        if n_sub >= 5:
            emit_l1attn(b)
            prog.fence()
        if n_sub >= 6:
            emit_l1ffn(b, "m1")
            prog.fence()
        emit_exit(b, y_hbm)
        global LAST_MTAP_ROWS
        LAST_MTAP_ROWS = dict(b.mtap_rows)
        with nc.Block() as block:
            prog.emit(block)
    return nc


def make_consts():
    f = np.arange(P, dtype=np.float32)[None, :]
    p = np.arange(P, dtype=np.float32)[:, None]
    ident = (f == p).astype(np.float32)
    t = np.arange(T, dtype=np.float64)
    em = np.exp(-DEC_C * t).astype(np.float32)
    epv = np.exp(DEC_C * t).astype(np.float32)
    pj = np.arange(P, dtype=np.float64)
    corr = np.exp(-2.0 * DEC_C * np.maximum(0.0, pj[None, :] - pj[:, None])
                  ).astype(np.float32)
    p64 = np.arange(64)
    sel64 = (p64[:, None] % 8 == np.arange(8)[None, :])
    return {
        "ident_f32": ident,
        "ident_bf16": ident.astype(ml_dtypes.bfloat16),
        "edecm": np.broadcast_to(em[None, :], (P, T)).astype(ml_dtypes.bfloat16),
        "edecp": np.broadcast_to(epv[None, :], (P, T)).astype(ml_dtypes.bfloat16),
        "corr": corr,
        "sel64": sel64.astype(ml_dtypes.bfloat16),
        "selcum": ((p64[:, None] % 8 == p64[None, :] % 8)
                   & (p64[:, None] // 8 < p64[None, :] // 8)
                   ).astype(ml_dtypes.bfloat16),
    }


def make_core_inputs(c, w):
    m = {}
    for n in WNAMES:
        arr = np.asarray(w[n], dtype=np.float32)
        if n in ("L1_ffn_wg", "L1_ffn_wu"):
            arr = arr.reshape(8, D, 4096)[c]
        elif n == "L1_ffn_wd":
            arr = arr.reshape(8, 4096, D)[c]
        else:
            arr = arr.reshape(WSHAPES[n])
        m[n] = np.ascontiguousarray(arr)
    m.update(make_consts())
    m["coreid"] = np.full((P, 1), float(c), np.float32)
    p64 = np.arange(64)
    m["cummask"] = ((p64[:, None] // 8 < c) * (p64[:, None] % 8 ==
                    np.arange(8)[None, :])).astype(ml_dtypes.bfloat16)
    return m


def kernel(L0_attn_x, **w):
    x = np.ascontiguousarray(np.asarray(L0_attn_x, dtype=np.float32))
    nc = build_nc()
    in_maps = []
    for c in range(N_CORES):
        m = make_core_inputs(c, w)
        m["x"] = x[c]
        in_maps.append(m)
    res = run_bass_kernel_spmd(nc, in_maps, core_ids=list(range(N_CORES)))
    return np.stack([res.results[c]["y"] for c in range(N_CORES)], axis=0)
